# version 07
# baseline (speedup 1.0000x reference)
import numpy as np
import ml_dtypes
import concourse.bass as bass
import concourse.mybir as mybir
from concourse.bass_utils import run_bass_kernel_spmd

F32 = mybir.dt.float32
BF16 = mybir.dt.bfloat16
AF = mybir.ActivationFunctionType
ALU = mybir.AluOpType
AX = mybir.AxisListType

S = 2048
D = 1024
NCH = 8
TBN = 4
DL = 512
DS = 4096
NE = 8
BIGI = 1.0e6
BIGM = 1.0e30

ENGS = ("sync", "tensor", "vector", "scalar", "gpsimd")
DMA_ENGS = ("sync", "scalar", "gpsimd")


class Buf:
    __slots__ = ("name", "writer", "readers")

    def __init__(self, name):
        self.name = name
        self.writer = None
        self.readers = []

    def retag(self, *olds):
        for o in olds:
            if o.writer is not None:
                self.readers.append(o.writer)
            self.readers.extend(o.readers)
        return self


class Bld:
    def __init__(self, nc):
        self.nc = nc
        self.q = {e: [] for e in ENGS}
        self.semobj = {}
        self.ccnt = {e: 0 for e in ENGS}
        self.dcnt = {e: 0 for e in DMA_ENGS}
        for e in ENGS:
            self.semobj["c_" + e] = nc.alloc_semaphore("c_" + e)
        for e in DMA_ENGS:
            self.semobj["d_" + e] = nc.alloc_semaphore("d_" + e)
        self.waited = {e: {} for e in ENGS}
        self.log = []

    def step(self, e, emit, reads=(), writes=(), dma=False):
        deps = {}
        for blist, isw in ((reads, False), (writes, True)):
            for b in blist:
                if b.writer is not None:
                    k, v = b.writer
                    if deps.get(k, -1) < v:
                        deps[k] = v
                if isw:
                    for k, v in b.readers:
                        if deps.get(k, -1) < v:
                            deps[k] = v
        if dma:
            key = "d_" + e
            self.dcnt[e] += 1
            tok = (key, self.dcnt[e] * 16)
            incv = 16
        else:
            key = "c_" + e
            self.ccnt[e] += 1
            tok = (key, self.ccnt[e])
            incv = 1
        w = self.waited[e]
        waits = []
        for k, v in sorted(deps.items()):
            if w.get(k, 0) < v:
                waits.append((k, v))
                w[k] = v
        sem = self.semobj[key]

        def closure(eng, waits=waits, emit=emit, sem=sem, incv=incv):
            for k, v in waits:
                eng.wait_ge(self.semobj[k], v)
            last = emit(eng)
            last.then_inc(sem, incv)
        self.q[e].append(closure)
        if self.log is not None:
            self.log.append((e, tok, tuple(waits),
                             tuple(b.name for b in reads),
                             tuple(b.name for b in writes)))
        for b in reads:
            b.readers.append(tok)
        for b in writes:
            b.writer = tok
            b.readers = []
        return tok

    def raw(self, e, fn):
        self.q[e].append(fn)


PHASE_MARKS = {}
LAST_B = [None]


def build_program(taps=()):
    nc = bass.Bass("TRN2", target_bir_lowering=False, debug=False)
    B = Bld(nc)
    LAST_B[0] = B

    def din(name, shape, dt=F32):
        return nc.dram_tensor(name, shape, dt, kind="ExternalInput").ap()

    def dout(name, shape, dt=F32):
        return nc.dram_tensor(name, shape, dt, kind="ExternalOutput").ap()

    x = din("x", [S, D])
    wT = {n: din(n, [D, D]) for n in ("wkT", "wvT", "wrT", "woT")}
    gateT = din("gateT", [D, NE])
    fc1T = din("fc1T", [D, DL])
    fc2T = din("fc2T", [DL, D])
    upT = din("upT", [D, DS])
    downTr = din("downTr", [8, DS, 128])
    w1T = din("w1T", [NE, DL, DL])
    w2T = din("w2T", [NE, DL, DL])
    prm = {n: din(n, [128, 8]) for n in
           ("tdcm", "tfcm", "tmkcm", "tmvcm", "tmrcm", "cbbc")}
    identf = din("identf", [128, 128])
    identb = din("identb", [128, 128], BF16)
    ones128b = din("ones128b", [128, 128], BF16)
    onesrowb = din("onesrowb", [1, 128], BF16)
    iota8 = din("iota8", [128, 128])
    iotam = din("iotam", [128, 128])
    epscol = din("epscol", [128, 1])

    out = dout("out", [S, D])
    tap = {}
    if "xn" in taps:
        tap["xn"] = dout("t_xn", [NCH, 128, 2052], BF16)
    if "ek" in taps:
        tap["ek"] = dout("t_ek", [NCH, 128, S], BF16)
    if "wkv" in taps:
        tap["wkv"] = dout("t_wkv", [NCH, 128, S], BF16)
    if "ekv" in taps:
        tap["ekv"] = dout("t_ekv", [NCH, 128, S], BF16)
    if "scan7" in taps:
        tap["scan7A"] = dout("t_scan7A", [128, 2052])
        tap["scan7B"] = dout("t_scan7B", [128, 2052])
    if "nd7" in taps:
        tap["num7"] = dout("t_num7", [128, S], BF16)
        tap["den7"] = dout("t_den7", [128, S], BF16)
        tap["rden7"] = dout("t_rden7", [128, S])
    if "ro" in taps:
        tap["ro"] = dout("t_ro", [NCH, 128, S], BF16)
    if "x1" in taps:
        tap["x1"] = dout("t_x1", [D, S])
    if "xcm" in taps:
        tap["xcm"] = dout("t_xcm", [D, S])
    if "y2" in taps:
        tap["y2"] = dout("t_y2", [NCH, 128, S], BF16)
    if "wts" in taps:
        tap["wts"] = dout("t_wts", [128, 128])
    if "sctm" in taps:
        tap["sctm"] = dout("t_sctm", [128, 128])
    if "lat" in taps:
        tap["lat"] = dout("t_lat", [4, 128, S], BF16)
    if "comb" in taps:
        tap["comb"] = dout("t_comb", [4, 128, S], BF16)
    if "wbc0" in taps:
        tap["wbc0"] = dout("t_wbc0", [128, S], BF16)
    if "h1w0" in taps:
        tap["h1w0"] = dout("t_h1w0", [4, 128, S], BF16)
    if "he0" in taps:
        tap["he0"] = dout("t_he0", [4, 128, S], BF16)
    if "h1r00" in taps:
        tap["h1r00"] = dout("t_h1r00", [128, 512], BF16)
        tap["h1s00"] = dout("t_h1s00", [128, 512], BF16)
    if "w1sb" in taps:
        tap["w1sb"] = dout("t_w1sb", [128, 2048], BF16)

    xcm_d = nc.dram_tensor("xcm_d", [D, S], F32).ap()
    x1_d = nc.dram_tensor("x1_d", [D, S], F32).ap()
    upTb_d = nc.dram_tensor("upTb_d", [8, 8, 128, 4, 128], BF16).ap()
    downTb_d = nc.dram_tensor("downTb_d", [8, 32, 128, 128], BF16).ap()

    sb = nc.alloc_sbuf_tensor
    c_identf = sb("c_identf", [128, 128], F32)
    c_ones128b = sb("c_ones128b", [128, 128], BF16)
    c_onesrowb = sb("c_onesrowb", [1, 128], BF16)
    c_identb = sb("c_identb", [128, 128], BF16)
    c_iota8 = sb("c_iota8", [128, 128], F32)
    c_iotam = sb("c_iotam", [128, 128], F32)
    c_eps = sb("c_eps", [128, 1], F32)
    c_p = {n: sb("c_" + n, [128, 8], F32) for n in prm}
    c_ew = sb("c_ew", [128, 8], F32)
    c_eu = sb("c_eu", [128, 8], F32)

    P1 = sb("P1", [128, 16384], BF16)
    P2 = sb("P2", [128, 16416], BF16)
    P3 = sb("P3", [128, 16384], BF16)
    P5 = sb("P5", [128, 16384], BF16)
    P4 = sb("P4", [128, 8192], F32)
    PW = sb("PW", [128, 16896], BF16)
    PWb = PW.bitcast(BF16)

    pb = [nc.alloc_psum_tensor(f"pbank{i}", [128, 512], F32)
          for i in range(8)]
    pbuf = [Buf(f"pb{i}") for i in range(8)]

    b_const = Buf("const")

    def dma(e, dst, src, wbuf, rbufs=(), **kw):
        return B.step(e, lambda eng: eng.dma_start(out=dst, in_=src, **kw),
                      reads=list(rbufs), writes=[wbuf], dma=True)

    for dst, src in ((c_identf[:], identf), (c_identb[:], identb),
                     (c_ones128b[:], ones128b), (c_onesrowb[:], onesrowb),
                     (c_iota8[:], iota8), (c_iotam[:], iotam),
                     (c_eps[:], epscol)):
        dma("scalar", dst, src, b_const)
    for n in prm:
        dma("scalar", c_p[n][:], prm[n], b_const)
    B.step("scalar", lambda s: s.activation(c_eu[:], c_p["tdcm"][:], AF.Exp),
           reads=[b_const], writes=[b_const])
    B.step("scalar", lambda s: s.activation(c_ew[:], c_eu[:], AF.Exp,
                                            scale=-1.0),
           reads=[b_const], writes=[b_const])
    B.step("scalar", lambda s: s.activation(c_eu[:], c_p["tfcm"][:], AF.Exp),
           reads=[b_const], writes=[b_const])

    w_slot = [PWb[:, 0:8192], PWb[:, 8256:16448]]
    w_sbuf = [Buf("wslot0"), Buf("wslot1")]

    def load_wT(slot, name):
        src = wT[name].rearrange("(c p) e -> p c e", p=128)
        dst = w_slot[slot].rearrange("p (c e) -> p c e", e=1024)
        return dma("gpsimd", dst, src, w_sbuf[slot])

    load_wT(0, "wkT")
    load_wT(1, "wvT")

    upsrc = upT.rearrange("(d p) (g l e) -> g p d (l e)", p=128, g=8,
                          e=128)
    updst = upTb_d.rearrange("g d p l e -> g p d (l e)")
    dnsrc = downTr.rearrange("o (sc p) e -> o p sc e", p=128)
    dndst = downTb_d.rearrange("o sc p e -> o p sc e")
    b_upTb = Buf("upTb")
    b_dnTb = Buf("dnTb")

    def emit_casts(items):
        for kind, idx in items:
            if kind == "u":
                dma("gpsimd", updst[idx], upsrc[idx], b_upTb)
            else:
                dma("gpsimd", dndst[idx], dnsrc[idx], b_dnTb)
    emit_casts([("u", 0), ("u", 1), ("u", 2), ("u", 3)])


    PHASE_MARKS.clear()
    def mark(name):
        PHASE_MARKS[name] = {e: B.ccnt[e] + B.dcnt.get(e, 0) * 0
                             for e in ENGS}
        PHASE_MARKS[name] = {e: len(B.q[e]) for e in ENGS}
    mark('ph0_end')
    P1f = P1.bitcast(F32)
    P3f = P3.bitcast(F32)
    b_xcm_sb = [Buf(f"xcmsb{c}") for c in range(NCH)]
    b_xcm_cg = [[Buf(f"xcmsb{c}_{g}") for g in range(4)]
                for c in range(NCH)]

    def xcm_sb(c):
        pool = P1f if c < 4 else P3f
        return pool[:, (c % 4) * 2048:(c % 4 + 1) * 2048]

    b_xin = [Buf("xin0"), Buf("xin1")]
    xin_ap = [P4[:, 0:4096], P4[:, 4096:8192]]
    xcm_rows = xcm_d.rearrange("(c p) t -> c p t", p=128)
    xsrc = x.rearrange("(g i p) d -> g p i d", p=128, g=4)
    for g in range(4):
        bi = g % 2
        dma("sync", xin_ap[bi].rearrange("p (i d) -> p i d", d=1024),
            xsrc[g], b_xin[bi])
        for c in range(NCH):
            bk = c % 2

            def emtg(t, g=g, c=c, bi=bi, bk=bk):
                last = None
                for i in range(4):
                    last = t.transpose(
                        pb[bk][:, i * 128:(i + 1) * 128],
                        xin_ap[bi][:, i * 1024 + c * 128:
                                   i * 1024 + (c + 1) * 128],
                        c_identf[:])
                return last
            B.step("tensor", emtg, reads=[b_xin[bi], b_const],
                   writes=[pbuf[bk]])
            B.step("scalar", lambda s, g=g, c=c, bk=bk: s.copy(
                xcm_sb(c)[:, g * 512:(g + 1) * 512], pb[bk][:]),
                reads=[pbuf[bk]], writes=[b_xcm_cg[c][g]])

    b_xq = Buf("xq").retag(b_xin[0], b_xin[1])
    b_xbq = Buf("xbq").retag(b_xin[0], b_xin[1])
    b_stat = Buf("stat").retag(b_xin[0], b_xin[1])
    xq_ap = P4[:, 0:4096]
    xbq_ap = P4.bitcast(BF16)[:, 8192:12288]
    mbc_ap = P4[:, 6144:6656]
    rbc_ap = P4[:, 6656:7168]
    vtmp_ap = P4[:, 5632:6144]

    def emit_layernorm(src_d, b_src, dst_ap_fn, b_dst, b_sqt, sqt_ap,
                       b_xbq=b_xbq, sb_src=None, sb_bufs=None,
                       xscr=None, b_xscr=None, banks=(6, 7)):
        if sb_src is None:
            src = src_d.rearrange("(c p) t -> p c t", p=128)
        for tb in range(TBN):
            if sb_src is None:
                dma("sync", xq_ap.rearrange("p (c t) -> p c t", t=512),
                    src[:, :, tb * 512:(tb + 1) * 512],
                    b_xq, rbufs=[b_src])

            def srcap(c, tb=tb):
                if sb_src is None:
                    return xq_ap[:, c * 512:(c + 1) * 512]
                return sb_src(c)[:, tb * 512:(tb + 1) * 512]

            def srcbuf(c, tb=tb):
                if sb_src is None:
                    return b_xq
                sel = sb_bufs[c]
                return sel[tb] if isinstance(sel, list) else sel
            for c in range(NCH):
                B.step("vector", lambda v, c=c, ap=srcap(c): v.tensor_copy(
                    xbq_ap[:, c * 512:(c + 1) * 512], ap),
                    reads=[srcbuf(c)], writes=[b_xbq])
                B.step("scalar", lambda s, c=c, ap=srcap(c): s.activation(
                    sqt_ap[:, c * 512:(c + 1) * 512], ap, AF.Square),
                    reads=[srcbuf(c)], writes=[b_sqt])

            bks, bkq = banks

            def mm_s1(t):
                last = None
                for c in range(NCH):
                    last = t.matmul(pb[bks][:], c_ones128b[:],
                                    xbq_ap[:, c * 512:(c + 1) * 512],
                                    start=(c == 0), stop=(c == NCH - 1))
                return last
            B.step("tensor", mm_s1, reads=[b_xbq, b_const],
                   writes=[pbuf[bks]])

            def mm_s2(t):
                last = None
                for c in range(NCH):
                    last = t.matmul(pb[bkq][:], c_ones128b[:],
                                    sqt_ap[:, c * 512:(c + 1) * 512],
                                    start=(c == 0), stop=(c == NCH - 1))
                return last
            B.step("tensor", mm_s2, reads=[b_sqt, b_const],
                   writes=[pbuf[bkq]])
            B.step("vector", lambda v, bks=bks: v.tensor_scalar_mul(
                mbc_ap, pb[bks][:], 1.0 / D), reads=[pbuf[bks]],
                writes=[b_stat])
            B.step("vector", lambda v, bkq=bkq: v.tensor_scalar_mul(
                rbc_ap, pb[bkq][:], 1.0 / D), reads=[pbuf[bkq]],
                writes=[b_stat])
            B.step("vector", lambda v: v.tensor_mul(
                vtmp_ap, mbc_ap, mbc_ap), reads=[b_stat], writes=[b_stat])
            B.step("vector", lambda v: v.tensor_sub(
                rbc_ap, rbc_ap, vtmp_ap), reads=[b_stat], writes=[b_stat])
            B.step("scalar", lambda s: s.activation(
                vtmp_ap, rbc_ap, AF.Ln, bias=c_eps[:, 0:1]),
                reads=[b_stat], writes=[b_stat])
            B.step("scalar", lambda s: s.activation(
                rbc_ap, vtmp_ap, AF.Exp, scale=-0.5),
                reads=[b_stat], writes=[b_stat])
            for c in range(NCH):
                if sb_src is None:
                    B.step("vector", lambda v, c=c: v.tensor_sub(
                        xq_ap[:, c * 512:(c + 1) * 512],
                        xq_ap[:, c * 512:(c + 1) * 512], mbc_ap),
                        reads=[b_xq, b_stat], writes=[b_xq])
                    B.step("vector", lambda v, c=c, tb=tb: v.tensor_mul(
                        dst_ap_fn(c, tb),
                        xq_ap[:, c * 512:(c + 1) * 512], rbc_ap),
                        reads=[b_xq, b_stat], writes=[b_dst])
                else:
                    B.step("gpsimd", lambda g, c=c, ap=srcap(c):
                           g.tensor_sub(xscr, ap, mbc_ap),
                           reads=[srcbuf(c), b_stat], writes=[b_xscr])
                    B.step("vector", lambda v, c=c, tb=tb: v.tensor_mul(
                        dst_ap_fn(c, tb), xscr, rbc_ap),
                        reads=[b_xscr, b_stat], writes=[b_dst])

    mark('a0_xpose')
    XNW = 2052
    b_xn = Buf("xn")
    xn_c = [P2[:, c * XNW:c * XNW + 2049] for c in range(NCH)]
    B.step("vector", lambda v: v.memset(P2[:, :], 0.0), writes=[b_xn])
    b_sqt1 = Buf("sqt1")
    sqt1_ap = P5[:, 8192:12288]

    b_xscr1 = Buf("xscr1").retag(b_xin[0], b_xin[1])
    emit_layernorm(None, None,
                   lambda c, tb: xn_c[c][:, 1 + tb * 512:1 + (tb + 1) * 512],
                   b_xn, b_sqt1, sqt1_ap, sb_src=xcm_sb, sb_bufs=b_xcm_cg,
                   xscr=P4[:, 0:512], b_xscr=b_xscr1)
    spill_tok = None
    for c in range(NCH):
        spill_tok = dma("scalar", xcm_rows[c], xcm_sb(c),
                        Buf(f"spill{c}"), rbufs=b_xcm_cg[c])
    b_xcm = Buf("xcm_d")
    b_xcm.writer = spill_tok
    if "xcm" in taps:
        dma("sync", tap["xcm"], xcm_d, Buf("tapxcm"), rbufs=[b_xcm])
    for c in range(NCH):
        pass
    if "xn" in taps:
        for c in range(NCH):
            dma("sync", tap["xn"][c], P2[:, c * XNW:(c + 1) * XNW],
                Buf(f"tapxn{c}"), rbufs=[b_xn])

    mark('a1_ln1')
    b_mix = Buf("mix").retag(*[b for c in range(4) for b in b_xcm_cg[c]])
    mix_c = [P1[:, c * 2048:(c + 1) * 2048] for c in range(NCH)]
    b_ek = Buf("ek").retag(b_sqt1,
                           *[b for c in range(4, 8) for b in b_xcm_cg[c]])
    ek_c = [P3[:, c * 2048:(c + 1) * 2048] for c in range(NCH)]
    b_ekv = Buf("ekv").retag(b_sqt1)
    ekv_c = [P5[:, c * 2048:(c + 1) * 2048] for c in range(NCH)]
    b_sig = Buf("sig").retag(b_xn)
    sig_c = [P2[:, c * 2048:(c + 1) * 2048] for c in range(NCH)]

    b_mixtb = [Buf(f"mixtb{tb}").retag(b_mix) for tb in range(TBN)]

    def emit_mix_tb(tmcol, tb):
        for c in range(NCH):
            xncur = xn_c[c][:, 1 + tb * 512:1 + (tb + 1) * 512]
            xnsh = xn_c[c][:, tb * 512:(tb + 1) * 512]
            mslice = mix_c[c][:, tb * 512:(tb + 1) * 512]
            B.step("gpsimd", lambda g, m=mslice, a=xncur, b=xnsh:
                   g.tensor_sub(m, a, b),
                   reads=[b_xn], writes=[b_mixtb[tb]])
            B.step("vector", lambda v, m=mslice, b=xnsh, c=c, tmcol=tmcol:
                   v.scalar_tensor_tensor(m, m, tmcol[:, c:c + 1], b,
                                          ALU.mult, ALU.add),
                   reads=[b_mixtb[tb], b_xn, b_const], writes=[b_mixtb[tb]])

    def emit_mix(tmcol):
        for tb in range(TBN):
            emit_mix_tb(tmcol, tb)

    def emit_proj(slot, post, banks):
        for tb in range(TBN):
            for e in range(NCH):
                bi = banks[(tb * NCH + e) % len(banks)]

                def mmf(t, e=e, tb=tb, bi=bi):
                    last = None
                    for c in range(NCH):
                        last = t.matmul(
                            pb[bi][:],
                            w_slot[slot][:, c * 1024 + e * 128:
                                         c * 1024 + (e + 1) * 128],
                            mix_c[c][:, tb * 512:(tb + 1) * 512],
                            start=(c == 0), stop=(c == NCH - 1))
                    return last
                B.step("tensor", mmf, reads=[w_sbuf[slot], b_mixtb[tb]],
                       writes=[pbuf[bi]])
                post(e, tb, bi)

    emit_mix(c_p["tmkcm"])

    def post_k(e, tb, bi):
        B.step("scalar", lambda s, e=e, tb=tb, bi=bi: s.activation(
            ek_c[e][:, tb * 512:(tb + 1) * 512], pb[bi][:], AF.Exp),
            reads=[pbuf[bi]], writes=[b_ek])
    emit_proj(0, post_k, (0, 1))
    if "ek" in taps:
        for c in range(NCH):
            dma("sync", tap["ek"][c], ek_c[c], Buf(f"tapek{c}"),
                rbufs=[b_ek])

    emit_mix(c_p["tmvcm"])
    load_wT(0, "wrT")
    emit_casts([("u", 4), ("u", 5), ("u", 6), ("u", 7)])

    def post_v(e, tb, bi):
        B.step("vector", lambda v, e=e, tb=tb, bi=bi: v.tensor_mul(
            ekv_c[e][:, tb * 512:(tb + 1) * 512],
            ek_c[e][:, tb * 512:(tb + 1) * 512], pb[bi][:]),
            reads=[pbuf[bi], b_ek], writes=[b_ekv])
    emit_proj(1, post_v, (2, 3))

    emit_mix(c_p["tmrcm"])
    load_wT(1, "woT")
    emit_casts([("d", o) for o in range(8)])
    b_upTb.writer = ("d_gpsimd", B.dcnt["gpsimd"] * 16)
    b_dnTb.writer = ("d_gpsimd", B.dcnt["gpsimd"] * 16)
    if "ekv" in taps:
        for c in range(NCH):
            dma("sync", tap["ekv"][c], ekv_c[c], Buf(f"tapekv{c}"),
                rbufs=[b_ekv])
    b_scanA = Buf("scanA").retag(b_xq, b_xbq, b_xin[0], b_xscr1)
    b_scanB = Buf("scanB").retag(b_xq, b_xbq, b_xin[0], b_xscr1)
    scanA = P4[:, 0:2052]
    scanB = P4[:, 2052:4104]
    B.step("vector", lambda v: v.memset(scanA[:, 0:1], 0.0),
           writes=[b_scanA])
    B.step("vector", lambda v: v.memset(scanB[:, 0:1], 0.0),
           writes=[b_scanB])
    b_wkv = Buf("wkv")
    for c in range(NCH):
        ewc = c_ew[:, c:c + 1]
        euc = c_eu[:, c:c + 1]
        B.step("vector", lambda v, c=c, ewc=ewc: v.tensor_tensor_scan(
            scanA[:, 1:2049], ewc.broadcast_to([128, 2048]), ekv_c[c],
            0.0, ALU.mult, ALU.add),
            reads=[b_ekv, b_const], writes=[b_scanA])
        B.step("vector", lambda v, c=c, ewc=ewc: v.tensor_tensor_scan(
            scanB[:, 1:2049], ewc.broadcast_to([128, 2048]), ek_c[c],
            0.0, ALU.mult, ALU.add),
            reads=[b_ek, b_const], writes=[b_scanB])
        B.step("vector", lambda v, c=c, euc=euc: v.scalar_tensor_tensor(
            ekv_c[c], ekv_c[c], euc, scanA[:, 0:2048], ALU.mult, ALU.add),
            reads=[b_ekv, b_scanA, b_const], writes=[b_ekv])
        B.step("vector", lambda v, c=c, euc=euc: v.scalar_tensor_tensor(
            ek_c[c], ek_c[c], euc, scanB[:, 0:2048], ALU.mult, ALU.add),
            reads=[b_ek, b_scanB, b_const], writes=[b_ek])
        if c == 7 and "nd7" in taps:
            dma("sync", tap["num7"], ekv_c[c], Buf("tapnum7"),
                rbufs=[b_ekv])
            dma("sync", tap["den7"], ek_c[c], Buf("tapden7"),
                rbufs=[b_ek])
        if c == 7 and "scan7" in taps:
            dma("sync", tap["scan7A"], scanA, Buf("tapscan7A"),
                rbufs=[b_scanA])
            dma("sync", tap["scan7B"], scanB, Buf("tapscan7B"),
                rbufs=[b_scanB])
        B.step("scalar", lambda s, c=c: s.activation(
            scanB[:, 1:2049], ek_c[c], AF.Ln),
            reads=[b_ek], writes=[b_scanB])
        B.step("scalar", lambda s: s.activation(
            scanA[:, 1:2049], scanB[:, 1:2049], AF.Exp, scale=-1.0),
            reads=[b_scanB, b_scanA], writes=[b_scanA])
        B.step("scalar", lambda s: s.copy(
            scanB[:, 0:8], scanA[:, 2041:2049]),
            reads=[b_scanA], writes=[b_scanB])
        if c == 7 and "nd7" in taps:
            dma("sync", tap["rden7"], scanA[:, 0:2048], Buf("taprd7"),
                rbufs=[b_scanA])
        B.step("vector", lambda v, c=c: v.tensor_mul(
            ekv_c[c], ekv_c[c], scanA[:, 1:2049]),
            reads=[b_ekv, b_scanA, b_scanB], writes=[b_wkv])

    b_rt = Buf("rt").retag(b_xbq)
    rt_ap = P4[:, 4096:4608]
    rt2_ap = P4[:, 4608:5120]

    def post_r(e, tb, bi):
        B.step("scalar", lambda s, bi=bi: s.activation(
            rt_ap, pb[bi][:], AF.Exp, scale=-1.0),
            reads=[pbuf[bi]], writes=[b_rt])
        B.step("scalar", lambda s: s.activation(
            rt2_ap, rt_ap, AF.Ln, bias=1.0),
            reads=[b_rt], writes=[b_rt])
        B.step("scalar", lambda s, e=e, tb=tb: s.activation(
            sig_c[e][:, tb * 512:(tb + 1) * 512], rt2_ap, AF.Exp,
            scale=-1.0),
            reads=[b_rt], writes=[b_sig])
    emit_proj(0, post_r, (4, 5))


    mark('a4_proj')
    b_ro = Buf("ro").retag(b_sig)
    ro_c = sig_c
    for c in range(NCH):
        if "wkv" in taps:
            dma("sync", tap["wkv"][c], ekv_c[c], Buf(f"tapwkv{c}"),
                rbufs=[b_wkv])
        B.step("vector", lambda v, c=c: v.tensor_mul(
            ro_c[c], ekv_c[c], ro_c[c]),
            reads=[b_wkv, b_sig, b_ro], writes=[b_ro])
    if "ro" in taps:
        for c in range(NCH):
            dma("sync", tap["ro"][c], ro_c[c], Buf(f"tapro{c}"),
                rbufs=[b_ro])

    mark('a5_wkv')
    P5f = P5.bitcast(F32)
    b_xP = [[Buf(f"x1sb{c}_{tb}") for tb in range(TBN)]
            for c in range(NCH)]
    for c in range(NCH):
        rt = [b_ek] if c < 4 else [b_ekv, b_wkv, b_sqt1]
        for tb in range(TBN):
            b_xP[c][tb].retag(*rt)

    def xP(c):
        pool = P3f if c < 4 else P5f
        return pool[:, (c % 4) * 2048:(c % 4 + 1) * 2048]

    b_xld = [Buf(f"xld{o}") for o in range(NCH)]
    for o in range(NCH):
        b_xld[o].retag(*b_xP[o])
        dma("sync", xP(o), xcm_rows[o], b_xld[o], rbufs=[b_xcm])
    x1_rows = x1_d.rearrange("(c p) t -> c p t", p=128)
    b_x1 = Buf("x1_d")
    last_x1_tok = None
    for tb in range(TBN):
        for o in range(NCH):
            bk = 6 + o % 2

            def mmo(t, o=o, tb=tb, bk=bk):
                last = None
                for c in range(NCH):
                    last = t.matmul(
                        pb[bk][:],
                        w_slot[1][:, c * 1024 + o * 128:
                                  c * 1024 + (o + 1) * 128],
                        ro_c[c][:, tb * 512:(tb + 1) * 512],
                        start=(c == 0), stop=(c == NCH - 1))
                return last
            B.step("tensor", mmo, reads=[w_sbuf[1], b_ro], writes=[pbuf[bk]])
            B.step("vector", lambda v, o=o, tb=tb, bk=bk: v.tensor_add(
                xP(o)[:, tb * 512:(tb + 1) * 512],
                xP(o)[:, tb * 512:(tb + 1) * 512], pb[bk][:]),
                reads=[b_xld[o], pbuf[bk]], writes=[b_xP[o][tb]])
            if tb == TBN - 1:
                last_x1_tok = dma("scalar", x1_rows[o], xP(o),
                                  Buf(f"x1w{o}"), rbufs=b_xP[o])
    b_x1.writer = last_x1_tok

    if "x1" in taps:
        dma("sync", tap["x1"], x1_d, Buf("tapx1"), rbufs=[b_x1])

    mark('a6_wo')
    b_y2 = Buf("y2").retag(*b_mixtb)
    y2_c = [P1[:, c * 2048:(c + 1) * 2048] for c in range(NCH)]
    b_xbq2 = Buf("xbq2").retag(b_xbq, b_rt)
    b_sqt2 = Buf("sqt2").retag(b_ro)
    sqt2_ap = P2[:, 0:4096]
    b_xscr2 = Buf("xscr2").retag(b_xscr1, b_scanA)
    emit_layernorm(None, None,
                   lambda c, tb: y2_c[c][:, tb * 512:(tb + 1) * 512],
                   b_y2, b_sqt2, sqt2_ap, b_xbq=b_xbq2,
                   sb_src=xP, sb_bufs=b_xP,
                   xscr=P4[:, 0:512], b_xscr=b_xscr2, banks=(4, 5))
    if "y2" in taps:
        for c in range(NCH):
            dma("sync", tap["y2"][c], y2_c[c], Buf(f"tapy2{c}"),
                rbufs=[b_y2])

    b_gw = Buf("gatew").retag(w_sbuf[0])
    g_ap = PWb[:, 0:64]
    dma("gpsimd", g_ap.rearrange("p (c e) -> p c e", e=8),
        gateT.rearrange("(c p) e -> p c e", p=128), b_gw)
    b_fc1 = Buf("fc1w").retag(w_sbuf[0])
    fc1_ap = PWb[:, 64:4160]
    dma("gpsimd", fc1_ap.rearrange("p (c l) -> p c l", l=512),
        fc1T.rearrange("(c p) l -> p c l", p=128), b_fc1)
    b_fc2 = Buf("fc2w").retag(w_sbuf[0])
    fc2_ap = PWb[:, 4160:8256]
    dma("gpsimd", fc2_ap.rearrange("p (l d) -> p l d", d=1024),
        fc2T.rearrange("(l p) d -> p l d", p=128), b_fc2)
    w1_ap = [PWb[:, 8256:10304], PWb[:, 12352:14400]]
    w2_ap = [PWb[:, 10304:12352], PWb[:, 14400:16448]]
    w1_sb = [Buf("w1s0").retag(w_sbuf[1]), Buf("w1s1").retag(w_sbuf[1])]
    w2_sb = [Buf("w2s0").retag(w_sbuf[1]), Buf("w2s1").retag(w_sbuf[1])]
    w1src = w1T.rearrange("e (l p) i -> e p l i", p=128)
    w2src = w2T.rearrange("e (i p) l -> e p i l", p=128)

    def load_expert(e):
        si = e % 2
        dma("gpsimd", w1_ap[si].rearrange("p (l i) -> p l i", i=512),
            w1src[e], w1_sb[si])
        dma("gpsimd", w2_ap[si].rearrange("p (i l) -> p i l", l=512),
            w2src[e], w2_sb[si])
    load_expert(0)
    load_expert(1)

    sc_cm = P3.bitcast(F32)[:, 4096:6144]
    sct2 = P3.bitcast(F32)[:, 6144:6656]
    b_sc = Buf("sc").retag(b_ek, *b_xP[2], *b_xP[3])
    for tb in range(TBN):
        def mmg(t, tb=tb):
            last = None
            for c in range(NCH):
                last = t.matmul(pb[0][0:8, :], g_ap[:, c * 8:(c + 1) * 8],
                                y2_c[c][:, tb * 512:(tb + 1) * 512],
                                start=(c == 0), stop=(c == NCH - 1))
            return last
        B.step("tensor", mmg, reads=[b_gw, b_y2], writes=[pbuf[0]])
        B.step("scalar", lambda s, tb=tb: s.activation(
            sc_cm[0:8, tb * 512:(tb + 1) * 512], pb[0][0:8, :], AF.Exp,
            scale=-1.0), reads=[pbuf[0]], writes=[b_sc])
        B.step("scalar", lambda s, tb=tb: s.activation(
            sct2[0:8, 0:512],
            sc_cm[0:8, tb * 512:(tb + 1) * 512], AF.Ln, bias=1.0),
            reads=[b_sc], writes=[b_sc])
        B.step("scalar", lambda s, tb=tb: s.activation(
            sc_cm[0:8, tb * 512:(tb + 1) * 512],
            sct2[0:8, 0:512], AF.Exp, scale=-1.0),
            reads=[b_sc], writes=[b_sc])

    sc_tm = P4[:, 4096:4224]
    b_sctm = Buf("sc_tm").retag(b_rt, b_xbq2)
    for i in range(16):
        def emt(t, i=i):
            return t.transpose(pb[1][:, (i % 8) * 8:(i % 8) * 8 + 8],
                               sc_cm[0:8, i * 128:(i + 1) * 128],
                               c_identf[0:8, 0:8])
        B.step("tensor", emt, reads=[b_sc, b_const], writes=[pbuf[1]])
        if i % 8 == 7:
            B.step("scalar", lambda s, i=i: s.copy(
                sc_tm[:, (i - 7) * 8:(i + 1) * 8], pb[1][:, 0:64]),
                reads=[pbuf[1]], writes=[b_sctm])

    if "sctm" in taps:
        dma("sync", tap["sctm"], sc_tm, Buf("tapsctm"), rbufs=[b_sctm])
    wts_tm = P4[:, 4224:4352]
    b_wts = Buf("wts")
    sfc = P4[:, 4352:4480]
    r1 = P4[:, 4480:4608]
    r2 = P4[:, 4608:4736]
    m1 = P4[:, 4736:4752]
    i1 = P4[:, 4752:4768]
    gs0 = P4[:, 4768:4784]
    gs1 = P4[:, 4784:4800]
    dnm = P4[:, 4800:4816]
    b_r = Buf("router").retag(b_xq, b_xbq2)

    def v8(ap):
        return ap.rearrange("p (a b) -> p a b", b=8)

    def v4(ap):
        return ap.rearrange("p (a b) -> p a b", b=4)

    def bc(ap, n, w):
        return ap.unsqueeze(2).broadcast_to([128, n, w])

    def rv(fn, reads=(), writes=()):
        B.step("vector", fn, reads=list(reads) + [b_r, b_sctm, b_const],
               writes=list(writes) + [b_r])

    rv(lambda v: v.tensor_add(
        v8(sfc), v8(sc_tm),
        c_p["cbbc"][:, :].unsqueeze(1).broadcast_to([128, 16, 8])))

    def top1_mask(src_view, io_view, iom_view, n, w, out_mask_view):
        rv(lambda v: v.reduce_max(m1[:, 0:n], src_view, axis=AX.X))
        rv(lambda v: v.tensor_tensor(out_mask_view, src_view,
                                     bc(m1[:, 0:n], n, w), ALU.is_ge))
        rv(lambda v: v.tensor_tensor(r2[:, 0:n * w].rearrange(
            "p (a b) -> p a b", b=w), out_mask_view, iom_view, ALU.mult))
        rv(lambda v: v.tensor_scalar(r2[:, 0:n * w], r2[:, 0:n * w],
                                     BIGI, None, ALU.add))
        rv(lambda v: v.tensor_reduce(i1[:, 0:n], r2[:, 0:n * w].rearrange(
            "p (a b) -> p a b", b=w), axis=AX.X, op=ALU.min))
        rv(lambda v: v.tensor_tensor(out_mask_view, io_view,
                                     bc(i1[:, 0:n], n, w), ALU.is_equal))

    for gi, gdst in ((0, gs0), (1, gs1)):
        gview = v8(sfc)[:, :, gi * 4:(gi + 1) * 4]
        iov = v8(c_iota8[:, 0:128])[:, :, gi * 4:(gi + 1) * 4]
        iomv = v8(c_iotam[:, 0:128])[:, :, gi * 4:(gi + 1) * 4]
        f1v = v4(r1[:, 0:64])
        top1_mask(gview, iov, iomv, 16, 4, f1v)
        rv(lambda v, gview=gview, f1v=f1v: v.scalar_tensor_tensor(
            v4(r2[:, 0:64]), f1v, -BIGM, gview, ALU.mult, ALU.add))
        rv(lambda v: v.reduce_max(i1[:, 0:16], v4(r2[:, 0:64]), axis=AX.X))
        rv(lambda v, gdst=gdst: v.tensor_add(gdst[:, 0:16], m1[:, 0:16],
                                             i1[:, 0:16]))
    rv(lambda v: v.tensor_tensor(gs0[:, 0:16], gs0[:, 0:16], gs1[:, 0:16],
                                 ALU.is_ge))
    rv(lambda v: v.tensor_tensor(
        v8(sfc)[:, :, 0:4], v8(sfc)[:, :, 0:4], bc(gs0[:, 0:16], 16, 4),
        ALU.mult))
    rv(lambda v: v.tensor_scalar(gs1[:, 0:16], gs0[:, 0:16], -1.0, 1.0,
                                 ALU.mult, ALU.add))
    rv(lambda v: v.tensor_tensor(
        v8(sfc)[:, :, 4:8], v8(sfc)[:, :, 4:8], bc(gs1[:, 0:16], 16, 4),
        ALU.mult))
    f1v = v8(r1)
    top1_mask(v8(sfc), v8(c_iota8[:, 0:128]), v8(c_iotam[:, 0:128]),
              16, 8, f1v)
    rv(lambda v: v.scalar_tensor_tensor(v8(r2), f1v, -BIGM, v8(sfc),
                                        ALU.mult, ALU.add))
    f2v = v8(wts_tm)
    top1_mask(v8(r2), v8(c_iota8[:, 0:128]), v8(c_iotam[:, 0:128]),
              16, 8, f2v)
    rv(lambda v: v.tensor_add(v8(wts_tm), v8(wts_tm), f1v))
    rv(lambda v: v.tensor_mul(v8(wts_tm), v8(wts_tm), v8(sc_tm)))
    rv(lambda v: v.reduce_sum(dnm[:, 0:16], v8(wts_tm), axis=AX.X))
    rv(lambda v: v.tensor_scalar(dnm[:, 0:16], dnm[:, 0:16], 1e-20, None,
                                 ALU.add))
    B.step("scalar", lambda s: s.activation(dnm[:, 0:16], dnm[:, 0:16],
                                            AF.Ln),
           reads=[b_r], writes=[b_r])
    B.step("scalar", lambda s: s.activation(dnm[:, 0:16], dnm[:, 0:16],
                                            AF.Exp, scale=-1.0),
           reads=[b_r], writes=[b_r])
    wts_em = sfc.rearrange("p (e a) -> p a e", a=16)
    rv(lambda v: v.tensor_tensor(wts_em, v8(wts_tm),
                                 bc(dnm[:, 0:16], 16, 8), ALU.mult),
       writes=[b_wts])
    wEb = P4.bitcast(BF16)[:, 9632:9760]
    wcmT = P4.bitcast(BF16)[:, 9760:9888]
    B.step("vector", lambda v: v.tensor_copy(wEb[:, 0:128], sfc),
           reads=[b_wts], writes=[b_wts])
    B.step("tensor", lambda t: t.transpose(
        pb[1].bitcast(BF16)[:, 0:128], wEb[:, 0:128], c_identb[:]),
        reads=[b_wts, b_const], writes=[pbuf[1]])
    B.step("scalar", lambda s: s.copy(wcmT[:, 0:128],
                                      pb[1].bitcast(BF16)[:, 0:128]),
        reads=[pbuf[1]], writes=[b_wts])
    if "wts" in taps:
        dma("sync", tap["wts"], sfc, Buf("tapwts"), rbufs=[b_wts])

    mark('f1_router')
    b_lat = Buf("lat").retag(b_ro)
    lat_l = [P2[:, l * 2048:(l + 1) * 2048] for l in range(4)]
    for l in range(4):
        for tb in range(TBN):
            bk = 2 + (l * TBN + tb) % 2

            def mmf1(t, l=l, tb=tb, bk=bk):
                last = None
                for c in range(NCH):
                    last = t.matmul(
                        pb[bk][:],
                        fc1_ap[:, c * 512 + l * 128:c * 512 + (l + 1) * 128],
                        y2_c[c][:, tb * 512:(tb + 1) * 512],
                        start=(c == 0), stop=(c == NCH - 1))
                return last
            B.step("tensor", mmf1, reads=[b_fc1, b_y2], writes=[pbuf[bk]])
            B.step("scalar", lambda s, l=l, tb=tb, bk=bk: s.copy(
                lat_l[l][:, tb * 512:(tb + 1) * 512], pb[bk][:]),
                reads=[pbuf[bk]], writes=[b_lat])
    if "lat" in taps:
        for l in range(4):
            dma("sync", tap["lat"][l], lat_l[l], Buf(f"taplat{l}"),
                rbufs=[b_lat])

    mark('f1_fc1')
    b_comb = Buf("comb").retag(b_sc, *b_xP[0], *b_xP[1])
    comb_l = [P3[:, l * 2048:(l + 1) * 2048] for l in range(4)]
    b_h1w = Buf("h1w").retag(b_lat)
    h1w_i = [P2[:, 8192 + i * 2048:8192 + (i + 1) * 2048] for i in range(4)]
    P4b = P4.bitcast(BF16)
    b_wrowd = [Buf("wrowd0").retag(b_xq, b_r),
               Buf("wrowd1").retag(b_xq, b_r)]
    b_wbc = [Buf("wbc0b").retag(b_xq, b_r),
             Buf("wbc1b").retag(b_xq, b_r)]
    wrow_ap2 = [P4b[:, 0:2048], P4b[:, 2048:4096]]
    wbc_ap2 = [P4b[:, 4096:6144], P4b[:, 6144:8192]]
    tmps_ap = P3[:, 12288:16384]
    tmpA = [tmps_ap[:, j * 512:(j + 1) * 512] for j in range(4)]
    tmpB = [tmps_ap[:, 2048 + j * 512:2048 + (j + 1) * 512]
            for j in range(4)]
    _tmprt = [b_sc] + [b for tb in range(TBN) for b in (b_xP[2][tb],
                                                        b_xP[3][tb])]
    b_tmpA = [Buf(f"tmpA{j}").retag(*_tmprt) for j in range(4)]
    b_tmpB = [Buf(f"tmpB{j}").retag(*_tmprt) for j in range(4)]

    for e in range(NE):
        si = e % 2
        dma("sync", wrow_ap2[si][0:1, 0:2048],
            wcmT[16 * e:16 * e + 16, 0:128], b_wrowd[si], rbufs=[b_wts])
        for tb in range(TBN):
            def mmb(t, tb=tb, si=si):
                return t.matmul(pb[1][:], c_onesrowb[:],
                                wrow_ap2[si][0:1, tb * 512:(tb + 1) * 512],
                                start=True, stop=True)
            B.step("tensor", mmb, reads=[b_wrowd[si], b_const],
                   writes=[pbuf[1]])
            B.step("scalar", lambda s, tb=tb, si=si: s.copy(
                wbc_ap2[si][:, tb * 512:(tb + 1) * 512], pb[1][:]),
                reads=[pbuf[1]], writes=[b_wbc[si]])
        if e == 0 and "w1sb" in taps:
            dma("sync", tap["w1sb"], w1_ap[0], Buf("tapw1"),
                rbufs=[w1_sb[0]])
        if e == 0 and "wbc0" in taps:
            dma("sync", tap["wbc0"], wbc_ap2[0], Buf("tapwbc0"),
                rbufs=[b_wbc[0]])
        for i in range(4):
            for tb in range(TBN):
                bk = 4 + (i * TBN + tb) % 2

                def mmh(t, i=i, tb=tb, bk=bk, si=si):
                    last = None
                    for l in range(4):
                        last = t.matmul(
                            pb[bk][:],
                            w1_ap[si][:, l * 512 + i * 128:
                                      l * 512 + (i + 1) * 128],
                            lat_l[l][:, tb * 512:(tb + 1) * 512],
                            start=(l == 0), stop=(l == 3))
                    return last
                B.step("tensor", mmh, reads=[w1_sb[si], b_lat],
                       writes=[pbuf[bk]])
                j = (i * TBN + tb) % 4
                B.step("scalar", lambda s, bk=bk, j=j: s.activation(
                    tmpA[j], pb[bk][:], AF.Relu),
                    reads=[pbuf[bk]], writes=[b_tmpA[j]])
                if e == 0 and i == 0 and tb == 0 and "h1r00" in taps:
                    dma("sync", tap["h1r00"], tmpA[0], Buf("taph1r"),
                        rbufs=[b_tmpA[0]])
                B.step("vector", lambda v, j=j: v.tensor_mul(
                    tmpB[j], tmpA[j], tmpA[j]),
                    reads=[b_tmpA[j]], writes=[b_tmpB[j]])
                if e == 0 and i == 0 and tb == 0 and "h1r00" in taps:
                    dma("sync", tap["h1s00"], tmpB[0], Buf("taph1s"),
                        rbufs=[b_tmpB[0]])
                B.step("vector", lambda v, i=i, tb=tb, j=j, si=si:
                       v.tensor_mul(
                           h1w_i[i][:, tb * 512:(tb + 1) * 512], tmpB[j],
                           wbc_ap2[si][:, tb * 512:(tb + 1) * 512]),
                       reads=[b_tmpB[j], b_wbc[si]], writes=[b_h1w])
        if e == 0 and "h1w0" in taps:
            for i in range(4):
                dma("sync", tap["h1w0"][i], h1w_i[i], Buf(f"taph1w{i}"),
                    rbufs=[b_h1w])
        for l in range(4):
            for tb in range(TBN):
                bk = 6 + (l * TBN + tb) % 2

                def mmc(t, l=l, tb=tb, bk=bk, si=si):
                    last = None
                    for i in range(4):
                        last = t.matmul(
                            pb[bk][:],
                            w2_ap[si][:, i * 512 + l * 128:
                                      i * 512 + (l + 1) * 128],
                            h1w_i[i][:, tb * 512:(tb + 1) * 512],
                            start=(i == 0), stop=(i == 3))
                    return last
                B.step("tensor", mmc, reads=[w2_sb[si], b_h1w],
                       writes=[pbuf[bk]])
                if e == 0:
                    B.step("scalar", lambda s, l=l, tb=tb, bk=bk: s.copy(
                        comb_l[l][:, tb * 512:(tb + 1) * 512], pb[bk][:]),
                        reads=[pbuf[bk]], writes=[b_comb])
                else:
                    B.step("vector", lambda v, l=l, tb=tb, bk=bk:
                           v.tensor_add(
                               comb_l[l][:, tb * 512:(tb + 1) * 512],
                               comb_l[l][:, tb * 512:(tb + 1) * 512],
                               pb[bk][:]),
                           reads=[pbuf[bk], b_comb], writes=[b_comb])
        if e + 2 < NE:
            load_expert(e + 2)
    if "comb" in taps:
        for l in range(4):
            dma("sync", tap["comb"][l], comb_l[l], Buf(f"tapcomb{l}"),
                rbufs=[b_comb])

    mark('f1_experts')
    upg_ap = [PWb[:, 8256:12352], PWb[:, 12352:16448]]
    b_upg = [Buf("upg0").retag(w1_sb[0], w2_sb[0]),
             Buf("upg1").retag(w1_sb[1], w2_sb[1])]
    dng_ap = [PWb[:, 0:4096], P2[:, 8192:12288]]
    b_dng = [Buf("dng0").retag(b_gw, b_fc1),
             Buf("dng1").retag(b_h1w)]

    b_upq = Buf("upq").retag(b_ekv,
                             *[b for c in range(4, 8) for b in b_xP[c]])
    upq_ds = [P5[:, ds * 512:(ds + 1) * 512] for ds in range(32)]
    b_fin = Buf("fin").retag(b_scanA, b_scanB, b_xq, b_wrowd[0],
                             b_wrowd[1], b_wbc[0], b_wbc[1], b_xscr2)
    fin_o = [P4[:, o * 512:(o + 1) * 512] for o in range(8)]
    b_x1q = Buf("x1q").retag(b_sctm, b_wts, b_r, b_rt)
    x1q_ap = [P4[:, 4096:4608], P4[:, 4608:5120]]
    b_ost2 = [Buf("ostage0").retag(b_stat), Buf("ostage1").retag(b_stat)]
    ost_ap2 = [P4[:, 5120:6144], P4[:, 6144:7168]]
    b_tmp3 = Buf("f2tmp").retag(*b_tmpA, *b_tmpB)
    tmp3 = sqt2_ap[:, 1024:1536]

    upg_src = upTb_d.rearrange("g d p l e -> g p d (l e)")
    dng_src = downTb_d.rearrange("o sc p e -> o p sc e")

    last_out_tok = None
    for tb in range(TBN):
        for g in range(8):
            gb = g % 2
            dma("sync",
                upg_ap[gb].rearrange("p (d le) -> p d le", le=512),
                upg_src[g], b_upg[gb], rbufs=[b_upTb])
            for l in range(4):
                ds = g * 4 + l
                bk = (ds % 2)

                def mmu(t, g=g, l=l, tb=tb, bk=bk, gb=gb):
                    last = None
                    for dd in range(8):
                        last = t.matmul(
                            pb[bk][:],
                            upg_ap[gb][:, (dd * 4 + l) * 128:
                                       (dd * 4 + l + 1) * 128],
                            y2_c[dd][:, tb * 512:(tb + 1) * 512],
                            start=(dd == 0), stop=(dd == 7))
                    return last
                B.step("tensor", mmu, reads=[b_upg[gb], b_y2],
                       writes=[pbuf[bk]])
                B.step("scalar", lambda s, bk=bk: s.activation(
                    tmp3, pb[bk][:], AF.Relu),
                    reads=[pbuf[bk]], writes=[b_tmp3])
                B.step("vector", lambda v, ds=ds: v.tensor_mul(
                    upq_ds[ds], tmp3, tmp3),
                    reads=[b_tmp3], writes=[b_upq])
        for o in range(8):
            ob = o % 2
            dma("sync",
                dng_ap[ob].rearrange("p (sc e) -> p sc e", e=128),
                dng_src[o], b_dng[ob], rbufs=[b_dnTb])
            bk = 2 + o % 2

            def mmd(t, o=o, tb=tb, bk=bk, ob=ob):
                last = None
                for l in range(4):
                    last = t.matmul(
                        pb[bk][:],
                        fc2_ap[:, l * 1024 + o * 128:
                               l * 1024 + (o + 1) * 128],
                        comb_l[l][:, tb * 512:(tb + 1) * 512],
                        start=(l == 0), stop=False)
                for sc in range(32):
                    last = t.matmul(
                        pb[bk][:],
                        dng_ap[ob][:, sc * 128:(sc + 1) * 128],
                        upq_ds[sc], start=False, stop=(sc == 31))
                return last
            B.step("tensor", mmd,
                   reads=[b_fc2, b_comb, b_dng[ob], b_upq],
                   writes=[pbuf[bk]])
            xi = o % 2
            dma("scalar", x1q_ap[xi],
                x1_rows[o][:, tb * 512:(tb + 1) * 512], b_x1q,
                rbufs=[b_x1])
            B.step("vector", lambda v, o=o, xi=xi, bk=bk: v.tensor_add(
                fin_o[o], x1q_ap[xi], pb[bk][:]),
                reads=[b_x1q, pbuf[bk]], writes=[b_fin])
        for ti in range(4):
            ob2 = ti % 2
            bk0 = 4 + 2 * ob2
            for o in range(8):
                bk = bk0 + (o // 4)

                def emt2(t, o=o, ti=ti, bk=bk):
                    return t.transpose(
                        pb[bk][:, (o % 4) * 128:(o % 4 + 1) * 128],
                        fin_o[o][:, ti * 128:(ti + 1) * 128],
                        c_identf[:])
                B.step("tensor", emt2, reads=[b_fin, b_const],
                       writes=[pbuf[bk]])
            B.step("scalar", lambda s, ob2=ob2, bk0=bk0: s.copy(
                ost_ap2[ob2][:, 0:512], pb[bk0][:]),
                reads=[pbuf[bk0]], writes=[b_ost2[ob2]])
            B.step("scalar", lambda s, ob2=ob2, bk0=bk0: s.copy(
                ost_ap2[ob2][:, 512:1024], pb[bk0 + 1][:]),
                reads=[pbuf[bk0 + 1]], writes=[b_ost2[ob2]])
            t0 = tb * 512 + ti * 128
            last_out_tok = dma("sync", out[t0:t0 + 128, :], ost_ap2[ob2],
                               Buf(f"outw{tb}_{ti}"), rbufs=[b_ost2[ob2]])

    mark('f2_shared')
    fin_sync = B.dcnt["sync"] * 16
    fin_scalar = B.dcnt["scalar"] * 16
    fin_gpsimd = B.dcnt["gpsimd"] * 16

    def final_waits(eng):
        eng.wait_ge(B.semobj["d_sync"], fin_sync)
        eng.wait_ge(B.semobj["d_scalar"], fin_scalar)
        eng.wait_ge(B.semobj["d_gpsimd"], fin_gpsimd)
    B.raw("sync", final_waits)

    with nc.Block() as block:
        @block.sync
        def _(eng):
            for fn in B.q["sync"]:
                fn(eng)

        @block.tensor
        def _(eng):
            for fn in B.q["tensor"]:
                fn(eng)

        @block.vector
        def _(eng):
            for fn in B.q["vector"]:
                fn(eng)

        @block.scalar
        def _(eng):
            for fn in B.q["scalar"]:
                fn(eng)

        @block.gpsimd
        def _(eng):
            for fn in B.q["gpsimd"]:
                fn(eng)

    return nc


def _prep_inputs(ins):
    f = np.ascontiguousarray
    w = {}
    w["wkT"] = f(np.asarray(ins["L0_attn_Wk"]).T)
    w["wvT"] = f(np.asarray(ins["L0_attn_Wv"]).T)
    w["wrT"] = f(np.asarray(ins["L0_attn_Wr"]).T)
    w["woT"] = f(np.asarray(ins["L0_attn_Wo"]).T)
    w["gateT"] = f(np.asarray(ins["L0_ffn_gate"]).T)
    w["fc1T"] = f(np.asarray(ins["L0_ffn_fc1_w"]).T)
    w["fc2T"] = f(np.asarray(ins["L0_ffn_fc2_w"]).T)
    w["upT"] = f(np.asarray(ins["L0_ffn_up_w"]).T)
    dw = np.asarray(ins["L0_ffn_down_w"])
    w["downTr"] = f(dw.reshape(8, 128, DS).transpose(0, 2, 1))
    w["w1T"] = f(np.asarray(ins["L0_ffn_w1"]).transpose(0, 2, 1))
    w["w2T"] = f(np.asarray(ins["L0_ffn_w2"]).transpose(0, 2, 1))
    for nm, key in (("tdcm", "L0_attn_time_decay"),
                    ("tfcm", "L0_attn_time_first")):
        w[nm] = f(np.asarray(key and ins[key]).reshape(8, 128).T)
    for nm, key in (("tmkcm", "L0_attn_time_mix_k"),
                    ("tmvcm", "L0_attn_time_mix_v"),
                    ("tmrcm", "L0_attn_time_mix_r")):
        w[nm] = f(np.asarray(ins[key]).reshape(8, 128).T)
    cb = np.asarray(ins["L0_ffn_corr_bias"]).astype(np.float32)
    w["cbbc"] = f(np.broadcast_to(cb[None, :], (128, 8)))
    w["identf"] = np.eye(128, dtype=np.float32)
    w["identb"] = np.eye(128, dtype=ml_dtypes.bfloat16)
    w["ones128b"] = np.ones((128, 128), dtype=ml_dtypes.bfloat16)
    w["onesrowb"] = np.ones((1, 128), dtype=ml_dtypes.bfloat16)
    io = np.tile(np.arange(8, dtype=np.float32), 16)[None, :]
    w["iota8"] = f(np.broadcast_to(io, (128, 128)))
    w["iotam"] = f(np.broadcast_to(io - BIGI, (128, 128)))
    w["epscol"] = np.full((128, 1), 1e-5, dtype=np.float32)
    return w


def run(inputs, taps=(), cores=8):
    nc = build_program(taps)
    shared = _prep_inputs(inputs)
    xfull = np.asarray(inputs["L0_attn_x"], dtype=np.float32)
    in_maps = []
    for b in range(cores):
        m = dict(shared)
        m["x"] = np.ascontiguousarray(xfull[b])
        in_maps.append(m)
    res = run_bass_kernel_spmd(nc, in_maps, core_ids=list(range(cores)))
    return res


def kernel(**inputs):
    res = run(inputs, taps=(), cores=8)
    return np.stack([r["out"] for r in res.results]).astype(np.float32)


# version 08
# speedup vs baseline: 1.0448x; 1.0448x over previous
import numpy as np
import ml_dtypes
import concourse.bass as bass
import concourse.mybir as mybir
from concourse.bass_utils import run_bass_kernel_spmd

F32 = mybir.dt.float32
BF16 = mybir.dt.bfloat16
AF = mybir.ActivationFunctionType
ALU = mybir.AluOpType
AX = mybir.AxisListType

S = 2048
D = 1024
NCH = 8
TBN = 4
DL = 512
DS = 4096
NE = 8
BIGI = 1.0e6
BIGM = 1.0e30

ENGS = ("sync", "tensor", "vector", "scalar", "gpsimd")
DMA_ENGS = ("sync", "scalar", "gpsimd")


class Buf:
    __slots__ = ("name", "writer", "readers")

    def __init__(self, name):
        self.name = name
        self.writer = None
        self.readers = []

    def retag(self, *olds):
        for o in olds:
            if o.writer is not None:
                self.readers.append(o.writer)
            self.readers.extend(o.readers)
        return self


class Bld:
    def __init__(self, nc):
        self.nc = nc
        self.q = {e: [] for e in ENGS}
        self.semobj = {}
        self.ccnt = {e: 0 for e in ENGS}
        self.dcnt = {e: 0 for e in DMA_ENGS}
        for e in ENGS:
            self.semobj["c_" + e] = nc.alloc_semaphore("c_" + e)
        for e in DMA_ENGS:
            self.semobj["d_" + e] = nc.alloc_semaphore("d_" + e)
        self.waited = {e: {} for e in ENGS}
        self.log = []

    def step(self, e, emit, reads=(), writes=(), dma=False):
        deps = {}
        for blist, isw in ((reads, False), (writes, True)):
            for b in blist:
                if b.writer is not None:
                    k, v = b.writer
                    if deps.get(k, -1) < v:
                        deps[k] = v
                if isw:
                    for k, v in b.readers:
                        if deps.get(k, -1) < v:
                            deps[k] = v
        if dma:
            key = "d_" + e
            self.dcnt[e] += 1
            tok = (key, self.dcnt[e] * 16)
            incv = 16
        else:
            key = "c_" + e
            self.ccnt[e] += 1
            tok = (key, self.ccnt[e])
            incv = 1
        w = self.waited[e]
        waits = []
        for k, v in sorted(deps.items()):
            if w.get(k, 0) < v:
                waits.append((k, v))
                w[k] = v
        sem = self.semobj[key]

        def closure(eng, waits=waits, emit=emit, sem=sem, incv=incv):
            for k, v in waits:
                eng.wait_ge(self.semobj[k], v)
            last = emit(eng)
            last.then_inc(sem, incv)
        self.q[e].append(closure)
        if self.log is not None:
            self.log.append((e, tok, tuple(waits),
                             tuple(b.name for b in reads),
                             tuple(b.name for b in writes)))
        for b in reads:
            b.readers.append(tok)
        for b in writes:
            b.writer = tok
            b.readers = []
        return tok

    def raw(self, e, fn):
        self.q[e].append(fn)


PHASE_MARKS = {}
LAST_B = [None]


def build_program(taps=()):
    nc = bass.Bass("TRN2", target_bir_lowering=False, debug=False)
    B = Bld(nc)
    LAST_B[0] = B

    def din(name, shape, dt=F32):
        return nc.dram_tensor(name, shape, dt, kind="ExternalInput").ap()

    def dout(name, shape, dt=F32):
        return nc.dram_tensor(name, shape, dt, kind="ExternalOutput").ap()

    x = din("x", [S, D])
    wT = {n: din(n, [D, D]) for n in ("wkT", "wvT", "wrT", "woT")}
    gateT = din("gateT", [D, NE])
    fc1T = din("fc1T", [D, DL])
    fc2T = din("fc2T", [DL, D])
    upT = din("upT", [D, DS])
    downTr = din("downTr", [8, DS, 128])
    w1T = din("w1T", [NE, DL, DL])
    w2T = din("w2T", [NE, DL, DL])
    prm = {n: din(n, [128, 8]) for n in
           ("tdcm", "tfcm", "tmkcm", "tmvcm", "tmrcm", "cbbc")}
    identf = din("identf", [128, 128])
    identb = din("identb", [128, 128], BF16)
    ones128b = din("ones128b", [128, 128], BF16)
    onesrowb = din("onesrowb", [1, 128], BF16)
    iota8 = din("iota8", [128, 128])
    iotam = din("iotam", [128, 128])
    epscol = din("epscol", [128, 1])

    out = dout("out", [S, D])
    tap = {}
    if "xn" in taps:
        tap["xn"] = dout("t_xn", [NCH, 128, 2052], BF16)
    if "ek" in taps:
        tap["ek"] = dout("t_ek", [NCH, 128, S], BF16)
    if "wkv" in taps:
        tap["wkv"] = dout("t_wkv", [NCH, 128, S], BF16)
    if "ekv" in taps:
        tap["ekv"] = dout("t_ekv", [NCH, 128, S], BF16)
    if "scan7" in taps:
        tap["scan7A"] = dout("t_scan7A", [128, 2052])
        tap["scan7B"] = dout("t_scan7B", [128, 2052])
    if "nd7" in taps:
        tap["num7"] = dout("t_num7", [128, S], BF16)
        tap["den7"] = dout("t_den7", [128, S], BF16)
        tap["rden7"] = dout("t_rden7", [128, S])
    if "ro" in taps:
        tap["ro"] = dout("t_ro", [NCH, 128, S], BF16)
    if "x1" in taps:
        tap["x1"] = dout("t_x1", [D, S])
    if "xcm" in taps:
        tap["xcm"] = dout("t_xcm", [D, S])
    if "y2" in taps:
        tap["y2"] = dout("t_y2", [NCH, 128, S], BF16)
    if "wts" in taps:
        tap["wts"] = dout("t_wts", [128, 128])
    if "sctm" in taps:
        tap["sctm"] = dout("t_sctm", [128, 128])
    if "lat" in taps:
        tap["lat"] = dout("t_lat", [4, 128, S], BF16)
    if "comb" in taps:
        tap["comb"] = dout("t_comb", [4, 128, S], BF16)
    if "wbc0" in taps:
        tap["wbc0"] = dout("t_wbc0", [128, S], BF16)
    if "h1w0" in taps:
        tap["h1w0"] = dout("t_h1w0", [4, 128, S], BF16)
    if "he0" in taps:
        tap["he0"] = dout("t_he0", [4, 128, S], BF16)
    if "h1r00" in taps:
        tap["h1r00"] = dout("t_h1r00", [128, 512], BF16)
        tap["h1s00"] = dout("t_h1s00", [128, 512], BF16)
    if "w1sb" in taps:
        tap["w1sb"] = dout("t_w1sb", [128, 2048], BF16)

    xcm_d = nc.dram_tensor("xcm_d", [D, S], F32).ap()
    x1_d = nc.dram_tensor("x1_d", [D, S], F32).ap()
    upTb_d = nc.dram_tensor("upTb_d", [8, 8, 128, 4, 128], BF16).ap()
    downTb_d = nc.dram_tensor("downTb_d", [8, 32, 128, 128], BF16).ap()

    sb = nc.alloc_sbuf_tensor
    c_identf = sb("c_identf", [128, 128], F32)
    c_ones128b = sb("c_ones128b", [128, 128], BF16)
    c_onesrowb = sb("c_onesrowb", [1, 128], BF16)
    c_identb = sb("c_identb", [128, 128], BF16)
    c_iota8 = sb("c_iota8", [128, 128], F32)
    c_iotam = sb("c_iotam", [128, 128], F32)
    c_eps = sb("c_eps", [128, 1], F32)
    c_p = {n: sb("c_" + n, [128, 8], F32) for n in prm}
    c_ew = sb("c_ew", [128, 8], F32)
    c_eu = sb("c_eu", [128, 8], F32)

    P1 = sb("P1", [128, 16384], BF16)
    P2 = sb("P2", [128, 16416], BF16)
    P3 = sb("P3", [128, 16384], BF16)
    P5 = sb("P5", [128, 16384], BF16)
    P4 = sb("P4", [128, 8192], F32)
    PW = sb("PW", [128, 16896], BF16)
    PWb = PW.bitcast(BF16)

    pb = [nc.alloc_psum_tensor(f"pbank{i}", [128, 512], F32)
          for i in range(8)]
    pbuf = [Buf(f"pb{i}") for i in range(8)]

    b_const = Buf("const")

    def dma(e, dst, src, wbuf, rbufs=(), **kw):
        return B.step(e, lambda eng: eng.dma_start(out=dst, in_=src, **kw),
                      reads=list(rbufs), writes=[wbuf], dma=True)

    for dst, src in ((c_identf[:], identf), (c_identb[:], identb),
                     (c_ones128b[:], ones128b), (c_onesrowb[:], onesrowb),
                     (c_iota8[:], iota8), (c_iotam[:], iotam),
                     (c_eps[:], epscol)):
        dma("scalar", dst, src, b_const)
    for n in prm:
        dma("scalar", c_p[n][:], prm[n], b_const)
    B.step("scalar", lambda s: s.activation(c_eu[:], c_p["tdcm"][:], AF.Exp),
           reads=[b_const], writes=[b_const])
    B.step("scalar", lambda s: s.activation(c_ew[:], c_eu[:], AF.Exp,
                                            scale=-1.0),
           reads=[b_const], writes=[b_const])
    B.step("scalar", lambda s: s.activation(c_eu[:], c_p["tfcm"][:], AF.Exp),
           reads=[b_const], writes=[b_const])

    w_slot = [PWb[:, 0:8192], PWb[:, 8256:16448]]
    w_sbuf = [Buf("wslot0"), Buf("wslot1")]

    def load_wT(slot, name):
        src = wT[name].rearrange("(c p) e -> p c e", p=128)
        dst = w_slot[slot].rearrange("p (c e) -> p c e", e=1024)
        return dma("gpsimd", dst, src, w_sbuf[slot])

    load_wT(0, "wkT")
    load_wT(1, "wvT")

    upsrc = upT.rearrange("(d p) (g l e) -> g p d (l e)", p=128, g=8,
                          e=128)
    updst = upTb_d.rearrange("g d p l e -> g p d (l e)")
    dnsrc = downTr.rearrange("o (sc p) e -> o p sc e", p=128)
    dndst = downTb_d.rearrange("o sc p e -> o p sc e")
    b_upTb = Buf("upTb")
    b_dnTb = Buf("dnTb")

    def emit_casts(items):
        for kind, idx in items:
            if kind == "u":
                dma("gpsimd", updst[idx], upsrc[idx], b_upTb)
            else:
                dma("gpsimd", dndst[idx], dnsrc[idx], b_dnTb)
    emit_casts([("u", 0), ("u", 1), ("u", 2), ("u", 3)])


    PHASE_MARKS.clear()
    def mark(name):
        PHASE_MARKS[name] = {e: B.ccnt[e] + B.dcnt.get(e, 0) * 0
                             for e in ENGS}
        PHASE_MARKS[name] = {e: len(B.q[e]) for e in ENGS}
    mark('ph0_end')
    P1f = P1.bitcast(F32)
    P3f = P3.bitcast(F32)
    b_xcm_sb = [Buf(f"xcmsb{c}") for c in range(NCH)]
    b_xcm_cg = [[Buf(f"xcmsb{c}_{g}") for g in range(4)]
                for c in range(NCH)]

    def xcm_sb(c):
        pool = P1f if c < 4 else P3f
        return pool[:, (c % 4) * 2048:(c % 4 + 1) * 2048]

    b_xin = [Buf("xin0"), Buf("xin1")]
    xin_ap = [P4[:, 0:2048], P4[:, 2048:4096]]
    xcm_rows = xcm_d.rearrange("(c p) t -> c p t", p=128)
    xsrc = x.rearrange("(g i p) d -> g p i d", p=128, g=8)
    for g in range(8):
        bi = g % 2
        dma("sync", xin_ap[bi].rearrange("p (i d) -> p i d", d=1024),
            xsrc[g], b_xin[bi])
        for c in range(NCH):
            bk = c % 2

            def emtg(t, g=g, c=c, bi=bi, bk=bk):
                last = None
                for i in range(2):
                    last = t.transpose(
                        pb[bk][:, (g % 2) * 256 + i * 128:
                               (g % 2) * 256 + (i + 1) * 128],
                        xin_ap[bi][:, i * 1024 + c * 128:
                                   i * 1024 + (c + 1) * 128],
                        c_identf[:])
                return last
            B.step("tensor", emtg, reads=[b_xin[bi], b_const],
                   writes=[pbuf[bk]])
            B.step("scalar", lambda s, g=g, c=c, bk=bk: s.copy(
                xcm_sb(c)[:, g * 256:(g + 1) * 256],
                pb[bk][:, (g % 2) * 256:(g % 2) * 256 + 256]),
                reads=[pbuf[bk]], writes=[b_xcm_cg[c][g // 2]])

    b_xq = Buf("xq").retag(b_xin[0], b_xin[1])
    b_xbq = Buf("xbq")
    b_stat = Buf("stat")
    xq_ap = P4[:, 0:4096]
    xbq_ap = P4.bitcast(BF16)[:, 8192:12288]
    mbc_ap = P4[:, 6144:6656]
    rbc_ap = P4[:, 6656:7168]
    vtmp_ap = P4[:, 5632:6144]

    def emit_layernorm(src_d, b_src, dst_ap_fn, b_dst, b_sqt, sqt_ap,
                       b_xbq=b_xbq, sb_src=None, sb_bufs=None,
                       xscr=None, b_xscr=None, banks=(6, 7)):
        if sb_src is None:
            src = src_d.rearrange("(c p) t -> p c t", p=128)
        for tb in range(TBN):
            if sb_src is None:
                dma("sync", xq_ap.rearrange("p (c t) -> p c t", t=512),
                    src[:, :, tb * 512:(tb + 1) * 512],
                    b_xq, rbufs=[b_src])

            def srcap(c, tb=tb):
                if sb_src is None:
                    return xq_ap[:, c * 512:(c + 1) * 512]
                return sb_src(c)[:, tb * 512:(tb + 1) * 512]

            def srcbuf(c, tb=tb):
                if sb_src is None:
                    return b_xq
                sel = sb_bufs[c]
                return sel[tb] if isinstance(sel, list) else sel
            for c in range(NCH):
                B.step("vector", lambda v, c=c, ap=srcap(c): v.tensor_copy(
                    xbq_ap[:, c * 512:(c + 1) * 512], ap),
                    reads=[srcbuf(c)], writes=[b_xbq])
                B.step("scalar", lambda s, c=c, ap=srcap(c): s.activation(
                    sqt_ap[:, c * 512:(c + 1) * 512], ap, AF.Square),
                    reads=[srcbuf(c)], writes=[b_sqt])

            bks, bkq = banks

            def mm_s1(t):
                last = None
                for c in range(NCH):
                    last = t.matmul(pb[bks][:], c_ones128b[:],
                                    xbq_ap[:, c * 512:(c + 1) * 512],
                                    start=(c == 0), stop=(c == NCH - 1))
                return last
            B.step("tensor", mm_s1, reads=[b_xbq, b_const],
                   writes=[pbuf[bks]])

            def mm_s2(t):
                last = None
                for c in range(NCH):
                    last = t.matmul(pb[bkq][:], c_ones128b[:],
                                    sqt_ap[:, c * 512:(c + 1) * 512],
                                    start=(c == 0), stop=(c == NCH - 1))
                return last
            B.step("tensor", mm_s2, reads=[b_sqt, b_const],
                   writes=[pbuf[bkq]])
            B.step("vector", lambda v, bks=bks: v.tensor_scalar_mul(
                mbc_ap, pb[bks][:], 1.0 / D), reads=[pbuf[bks]],
                writes=[b_stat])
            B.step("vector", lambda v, bkq=bkq: v.tensor_scalar_mul(
                rbc_ap, pb[bkq][:], 1.0 / D), reads=[pbuf[bkq]],
                writes=[b_stat])
            B.step("vector", lambda v: v.tensor_mul(
                vtmp_ap, mbc_ap, mbc_ap), reads=[b_stat], writes=[b_stat])
            B.step("vector", lambda v: v.tensor_sub(
                rbc_ap, rbc_ap, vtmp_ap), reads=[b_stat], writes=[b_stat])
            B.step("scalar", lambda s: s.activation(
                vtmp_ap, rbc_ap, AF.Ln, bias=c_eps[:, 0:1]),
                reads=[b_stat], writes=[b_stat])
            B.step("scalar", lambda s: s.activation(
                rbc_ap, vtmp_ap, AF.Exp, scale=-0.5),
                reads=[b_stat], writes=[b_stat])
            for c in range(NCH):
                if sb_src is None:
                    B.step("vector", lambda v, c=c: v.tensor_sub(
                        xq_ap[:, c * 512:(c + 1) * 512],
                        xq_ap[:, c * 512:(c + 1) * 512], mbc_ap),
                        reads=[b_xq, b_stat], writes=[b_xq])
                    B.step("vector", lambda v, c=c, tb=tb: v.tensor_mul(
                        dst_ap_fn(c, tb),
                        xq_ap[:, c * 512:(c + 1) * 512], rbc_ap),
                        reads=[b_xq, b_stat], writes=[b_dst])
                else:
                    B.step("vector", lambda v, c=c, ap=srcap(c):
                           v.tensor_sub(xscr, ap, mbc_ap),
                           reads=[srcbuf(c), b_stat], writes=[b_xscr])
                    B.step("vector", lambda v, c=c, tb=tb: v.tensor_mul(
                        dst_ap_fn(c, tb), xscr, rbc_ap),
                        reads=[b_xscr, b_stat], writes=[b_dst])

    mark('a0_xpose')
    XNW = 2052
    b_xn = Buf("xn")
    xn_c = [P2[:, c * XNW:c * XNW + 2049] for c in range(NCH)]
    B.step("vector", lambda v: v.memset(P2[:, :], 0.0), writes=[b_xn])
    b_sqt1 = Buf("sqt1")
    sqt1_ap = P5[:, 8192:12288]

    b_xscr1 = Buf("xscr1")
    emit_layernorm(None, None,
                   lambda c, tb: xn_c[c][:, 1 + tb * 512:1 + (tb + 1) * 512],
                   b_xn, b_sqt1, sqt1_ap, sb_src=xcm_sb, sb_bufs=b_xcm_cg,
                   xscr=P4[:, 7168:7680], b_xscr=b_xscr1)
    spill_tok = None
    for c in range(NCH):
        spill_tok = dma("scalar", xcm_rows[c], xcm_sb(c),
                        Buf(f"spill{c}"), rbufs=b_xcm_cg[c])
    b_xcm = Buf("xcm_d")
    b_xcm.writer = spill_tok
    if "xcm" in taps:
        dma("sync", tap["xcm"], xcm_d, Buf("tapxcm"), rbufs=[b_xcm])
    for c in range(NCH):
        pass
    if "xn" in taps:
        for c in range(NCH):
            dma("sync", tap["xn"][c], P2[:, c * XNW:(c + 1) * XNW],
                Buf(f"tapxn{c}"), rbufs=[b_xn])

    mark('a1_ln1')
    b_mix = Buf("mix").retag(*[b for c in range(4) for b in b_xcm_cg[c]])
    mix_c = [P1[:, c * 2048:(c + 1) * 2048] for c in range(NCH)]
    b_ek = Buf("ek").retag(b_sqt1,
                           *[b for c in range(4, 8) for b in b_xcm_cg[c]])
    ek_c = [P3[:, c * 2048:(c + 1) * 2048] for c in range(NCH)]
    b_ekv = Buf("ekv").retag(b_sqt1)
    ekv_c = [P5[:, c * 2048:(c + 1) * 2048] for c in range(NCH)]
    b_sig = Buf("sig").retag(b_xn)
    sig_c = [P2[:, c * 2048:(c + 1) * 2048] for c in range(NCH)]

    b_mixtb = [Buf(f"mixtb{tb}").retag(b_mix) for tb in range(TBN)]

    def emit_mix_tb(tmcol, tb):
        for c in range(NCH):
            xncur = xn_c[c][:, 1 + tb * 512:1 + (tb + 1) * 512]
            xnsh = xn_c[c][:, tb * 512:(tb + 1) * 512]
            mslice = mix_c[c][:, tb * 512:(tb + 1) * 512]
            B.step("vector", lambda v, m=mslice, a=xncur, b=xnsh:
                   v.tensor_sub(m, a, b),
                   reads=[b_xn], writes=[b_mixtb[tb]])
            B.step("vector", lambda v, m=mslice, b=xnsh, c=c, tmcol=tmcol:
                   v.scalar_tensor_tensor(m, m, tmcol[:, c:c + 1], b,
                                          ALU.mult, ALU.add),
                   reads=[b_mixtb[tb], b_xn, b_const], writes=[b_mixtb[tb]])

    def emit_mix(tmcol):
        for tb in range(TBN):
            emit_mix_tb(tmcol, tb)

    def emit_proj(slot, post, banks):
        for tb in range(TBN):
            for e in range(NCH):
                bi = banks[(tb * NCH + e) % len(banks)]

                def mmf(t, e=e, tb=tb, bi=bi):
                    last = None
                    for c in range(NCH):
                        last = t.matmul(
                            pb[bi][:],
                            w_slot[slot][:, c * 1024 + e * 128:
                                         c * 1024 + (e + 1) * 128],
                            mix_c[c][:, tb * 512:(tb + 1) * 512],
                            start=(c == 0), stop=(c == NCH - 1))
                    return last
                B.step("tensor", mmf, reads=[w_sbuf[slot], b_mixtb[tb]],
                       writes=[pbuf[bi]])
                post(e, tb, bi)

    emit_mix(c_p["tmkcm"])

    def post_k(e, tb, bi):
        B.step("scalar", lambda s, e=e, tb=tb, bi=bi: s.activation(
            ek_c[e][:, tb * 512:(tb + 1) * 512], pb[bi][:], AF.Exp),
            reads=[pbuf[bi]], writes=[b_ek])
    emit_proj(0, post_k, (0, 1))
    if "ek" in taps:
        for c in range(NCH):
            dma("sync", tap["ek"][c], ek_c[c], Buf(f"tapek{c}"),
                rbufs=[b_ek])

    emit_mix(c_p["tmvcm"])
    load_wT(0, "wrT")
    emit_casts([("u", 4), ("u", 5), ("u", 6), ("u", 7)])

    def post_v(e, tb, bi):
        B.step("vector", lambda v, e=e, tb=tb, bi=bi: v.tensor_mul(
            ekv_c[e][:, tb * 512:(tb + 1) * 512],
            ek_c[e][:, tb * 512:(tb + 1) * 512], pb[bi][:]),
            reads=[pbuf[bi], b_ek], writes=[b_ekv])
    emit_proj(1, post_v, (2, 3))

    emit_mix(c_p["tmrcm"])
    load_wT(1, "woT")
    emit_casts([("d", o) for o in range(8)])
    b_upTb.writer = ("d_gpsimd", B.dcnt["gpsimd"] * 16)
    b_dnTb.writer = ("d_gpsimd", B.dcnt["gpsimd"] * 16)
    if "ekv" in taps:
        for c in range(NCH):
            dma("sync", tap["ekv"][c], ekv_c[c], Buf(f"tapekv{c}"),
                rbufs=[b_ekv])
    b_scanA = Buf("scanA").retag(b_xq, b_xbq, b_xin[0], b_xscr1)
    b_scanB = Buf("scanB").retag(b_xq, b_xbq, b_xin[0], b_xscr1)
    scanA = P4[:, 0:2052]
    scanB = P4[:, 2052:4104]
    B.step("vector", lambda v: v.memset(scanA[:, 0:1], 0.0),
           writes=[b_scanA])
    B.step("vector", lambda v: v.memset(scanB[:, 0:1], 0.0),
           writes=[b_scanB])
    b_wkv = Buf("wkv")
    for c in range(NCH):
        ewc = c_ew[:, c:c + 1]
        euc = c_eu[:, c:c + 1]
        B.step("vector", lambda v, c=c, ewc=ewc: v.tensor_tensor_scan(
            scanA[:, 1:2049], ewc.broadcast_to([128, 2048]), ekv_c[c],
            0.0, ALU.mult, ALU.add),
            reads=[b_ekv, b_const], writes=[b_scanA])
        B.step("vector", lambda v, c=c, ewc=ewc: v.tensor_tensor_scan(
            scanB[:, 1:2049], ewc.broadcast_to([128, 2048]), ek_c[c],
            0.0, ALU.mult, ALU.add),
            reads=[b_ek, b_const], writes=[b_scanB])
        B.step("vector", lambda v, c=c, euc=euc: v.scalar_tensor_tensor(
            ekv_c[c], ekv_c[c], euc, scanA[:, 0:2048], ALU.mult, ALU.add),
            reads=[b_ekv, b_scanA, b_const], writes=[b_ekv])
        B.step("vector", lambda v, c=c, euc=euc: v.scalar_tensor_tensor(
            ek_c[c], ek_c[c], euc, scanB[:, 0:2048], ALU.mult, ALU.add),
            reads=[b_ek, b_scanB, b_const], writes=[b_ek])
        if c == 7 and "nd7" in taps:
            dma("sync", tap["num7"], ekv_c[c], Buf("tapnum7"),
                rbufs=[b_ekv])
            dma("sync", tap["den7"], ek_c[c], Buf("tapden7"),
                rbufs=[b_ek])
        if c == 7 and "scan7" in taps:
            dma("sync", tap["scan7A"], scanA, Buf("tapscan7A"),
                rbufs=[b_scanA])
            dma("sync", tap["scan7B"], scanB, Buf("tapscan7B"),
                rbufs=[b_scanB])
        B.step("scalar", lambda s, c=c: s.activation(
            scanB[:, 1:2049], ek_c[c], AF.Ln),
            reads=[b_ek], writes=[b_scanB])
        B.step("scalar", lambda s: s.activation(
            scanA[:, 1:2049], scanB[:, 1:2049], AF.Exp, scale=-1.0),
            reads=[b_scanB, b_scanA], writes=[b_scanA])
        B.step("scalar", lambda s: s.copy(
            scanB[:, 0:8], scanA[:, 2041:2049]),
            reads=[b_scanA], writes=[b_scanB])
        if c == 7 and "nd7" in taps:
            dma("sync", tap["rden7"], scanA[:, 0:2048], Buf("taprd7"),
                rbufs=[b_scanA])
        B.step("vector", lambda v, c=c: v.tensor_mul(
            ekv_c[c], ekv_c[c], scanA[:, 1:2049]),
            reads=[b_ekv, b_scanA, b_scanB], writes=[b_wkv])

    b_rt = Buf("rt").retag(b_xbq)
    rt_ap = P4[:, 4096:4608]
    rt2_ap = P4[:, 4608:5120]

    def post_r(e, tb, bi):
        B.step("scalar", lambda s, bi=bi: s.activation(
            rt_ap, pb[bi][:], AF.Exp, scale=-1.0),
            reads=[pbuf[bi]], writes=[b_rt])
        B.step("scalar", lambda s: s.activation(
            rt2_ap, rt_ap, AF.Ln, bias=1.0),
            reads=[b_rt], writes=[b_rt])
        B.step("scalar", lambda s, e=e, tb=tb: s.activation(
            sig_c[e][:, tb * 512:(tb + 1) * 512], rt2_ap, AF.Exp,
            scale=-1.0),
            reads=[b_rt], writes=[b_sig])
    emit_proj(0, post_r, (4, 5))


    mark('a4_proj')
    b_ro = Buf("ro").retag(b_sig)
    ro_c = sig_c
    for c in range(NCH):
        if "wkv" in taps:
            dma("sync", tap["wkv"][c], ekv_c[c], Buf(f"tapwkv{c}"),
                rbufs=[b_wkv])
        B.step("vector", lambda v, c=c: v.tensor_mul(
            ro_c[c], ekv_c[c], ro_c[c]),
            reads=[b_wkv, b_sig, b_ro], writes=[b_ro])
    if "ro" in taps:
        for c in range(NCH):
            dma("sync", tap["ro"][c], ro_c[c], Buf(f"tapro{c}"),
                rbufs=[b_ro])

    mark('a5_wkv')
    P5f = P5.bitcast(F32)
    b_xP = [[Buf(f"x1sb{c}_{tb}") for tb in range(TBN)]
            for c in range(NCH)]
    for c in range(NCH):
        rt = [b_ek] if c < 4 else [b_ekv, b_wkv, b_sqt1]
        for tb in range(TBN):
            b_xP[c][tb].retag(*rt)

    def xP(c):
        pool = P3f if c < 4 else P5f
        return pool[:, (c % 4) * 2048:(c % 4 + 1) * 2048]

    b_xld = [Buf(f"xld{o}") for o in range(NCH)]
    for o in range(NCH):
        b_xld[o].retag(*b_xP[o])
        dma("sync", xP(o), xcm_rows[o], b_xld[o], rbufs=[b_xcm])
    x1_rows = x1_d.rearrange("(c p) t -> c p t", p=128)
    b_x1 = Buf("x1_d")
    last_x1_tok = None
    for tb in range(TBN):
        for o in range(NCH):
            bk = 6 + o % 2

            def mmo(t, o=o, tb=tb, bk=bk):
                last = None
                for c in range(NCH):
                    last = t.matmul(
                        pb[bk][:],
                        w_slot[1][:, c * 1024 + o * 128:
                                  c * 1024 + (o + 1) * 128],
                        ro_c[c][:, tb * 512:(tb + 1) * 512],
                        start=(c == 0), stop=(c == NCH - 1))
                return last
            B.step("tensor", mmo, reads=[w_sbuf[1], b_ro], writes=[pbuf[bk]])
            B.step("vector", lambda v, o=o, tb=tb, bk=bk: v.tensor_add(
                xP(o)[:, tb * 512:(tb + 1) * 512],
                xP(o)[:, tb * 512:(tb + 1) * 512], pb[bk][:]),
                reads=[b_xld[o], pbuf[bk]], writes=[b_xP[o][tb]])
            if tb == TBN - 1:
                last_x1_tok = dma("scalar", x1_rows[o], xP(o),
                                  Buf(f"x1w{o}"), rbufs=b_xP[o])
    b_x1.writer = last_x1_tok

    if "x1" in taps:
        dma("sync", tap["x1"], x1_d, Buf("tapx1"), rbufs=[b_x1])

    mark('a6_wo')
    b_y2 = Buf("y2").retag(*b_mixtb)
    y2_c = [P1[:, c * 2048:(c + 1) * 2048] for c in range(NCH)]
    b_xbq2 = Buf("xbq2").retag(b_xbq, b_rt)
    b_sqt2 = Buf("sqt2").retag(b_ro)
    sqt2_ap = P2[:, 0:4096]
    b_xscr2 = Buf("xscr2").retag(b_xscr1, b_scanA)
    emit_layernorm(None, None,
                   lambda c, tb: y2_c[c][:, tb * 512:(tb + 1) * 512],
                   b_y2, b_sqt2, sqt2_ap, b_xbq=b_xbq2,
                   sb_src=xP, sb_bufs=b_xP,
                   xscr=P4[:, 7168:7680], b_xscr=b_xscr2, banks=(4, 5))
    if "y2" in taps:
        for c in range(NCH):
            dma("sync", tap["y2"][c], y2_c[c], Buf(f"tapy2{c}"),
                rbufs=[b_y2])

    b_gw = Buf("gatew").retag(w_sbuf[0])
    g_ap = PWb[:, 0:64]
    dma("gpsimd", g_ap.rearrange("p (c e) -> p c e", e=8),
        gateT.rearrange("(c p) e -> p c e", p=128), b_gw)
    b_fc1 = Buf("fc1w").retag(w_sbuf[0])
    fc1_ap = PWb[:, 64:4160]
    dma("gpsimd", fc1_ap.rearrange("p (c l) -> p c l", l=512),
        fc1T.rearrange("(c p) l -> p c l", p=128), b_fc1)
    b_fc2 = Buf("fc2w").retag(w_sbuf[0])
    fc2_ap = PWb[:, 4160:8256]
    dma("gpsimd", fc2_ap.rearrange("p (l d) -> p l d", d=1024),
        fc2T.rearrange("(l p) d -> p l d", p=128), b_fc2)
    w1_ap = [PWb[:, 8256:10304], PWb[:, 12352:14400]]
    w2_ap = [PWb[:, 10304:12352], PWb[:, 14400:16448]]
    w1_sb = [Buf("w1s0").retag(w_sbuf[1]), Buf("w1s1").retag(w_sbuf[1])]
    w2_sb = [Buf("w2s0").retag(w_sbuf[1]), Buf("w2s1").retag(w_sbuf[1])]
    w1src = w1T.rearrange("e (l p) i -> e p l i", p=128)
    w2src = w2T.rearrange("e (i p) l -> e p i l", p=128)

    def load_expert(e):
        si = e % 2
        dma("gpsimd", w1_ap[si].rearrange("p (l i) -> p l i", i=512),
            w1src[e], w1_sb[si])
        dma("gpsimd", w2_ap[si].rearrange("p (i l) -> p i l", l=512),
            w2src[e], w2_sb[si])
    load_expert(0)
    load_expert(1)

    sc_cm = P3.bitcast(F32)[:, 4096:6144]
    sct2 = P3.bitcast(F32)[:, 6144:6656]
    b_sc = Buf("sc").retag(b_ek, *b_xP[2], *b_xP[3])
    for tb in range(TBN):
        def mmg(t, tb=tb):
            last = None
            for c in range(NCH):
                last = t.matmul(pb[0][0:8, :], g_ap[:, c * 8:(c + 1) * 8],
                                y2_c[c][:, tb * 512:(tb + 1) * 512],
                                start=(c == 0), stop=(c == NCH - 1))
            return last
        B.step("tensor", mmg, reads=[b_gw, b_y2], writes=[pbuf[0]])
        B.step("scalar", lambda s, tb=tb: s.activation(
            sc_cm[0:8, tb * 512:(tb + 1) * 512], pb[0][0:8, :], AF.Exp,
            scale=-1.0), reads=[pbuf[0]], writes=[b_sc])
        B.step("scalar", lambda s, tb=tb: s.activation(
            sct2[0:8, 0:512],
            sc_cm[0:8, tb * 512:(tb + 1) * 512], AF.Ln, bias=1.0),
            reads=[b_sc], writes=[b_sc])
        B.step("scalar", lambda s, tb=tb: s.activation(
            sc_cm[0:8, tb * 512:(tb + 1) * 512],
            sct2[0:8, 0:512], AF.Exp, scale=-1.0),
            reads=[b_sc], writes=[b_sc])

    sc_tm = P4[:, 4096:4224]
    b_sctm = Buf("sc_tm").retag(b_rt, b_xbq2)
    for i in range(16):
        def emt(t, i=i):
            return t.transpose(pb[1][:, (i % 8) * 8:(i % 8) * 8 + 8],
                               sc_cm[0:8, i * 128:(i + 1) * 128],
                               c_identf[0:8, 0:8])
        B.step("tensor", emt, reads=[b_sc, b_const], writes=[pbuf[1]])
        if i % 8 == 7:
            B.step("scalar", lambda s, i=i: s.copy(
                sc_tm[:, (i - 7) * 8:(i + 1) * 8], pb[1][:, 0:64]),
                reads=[pbuf[1]], writes=[b_sctm])

    if "sctm" in taps:
        dma("sync", tap["sctm"], sc_tm, Buf("tapsctm"), rbufs=[b_sctm])
    wts_tm = P4[:, 4224:4352]
    b_wts = Buf("wts")
    sfc = P4[:, 4352:4480]
    r1 = P4[:, 4480:4608]
    r2 = P4[:, 4608:4736]
    m1 = P4[:, 4736:4752]
    i1 = P4[:, 4752:4768]
    gs0 = P4[:, 4768:4784]
    gs1 = P4[:, 4784:4800]
    dnm = P4[:, 4800:4816]
    b_r = Buf("router").retag(b_xq, b_xbq2)

    def v8(ap):
        return ap.rearrange("p (a b) -> p a b", b=8)

    def v4(ap):
        return ap.rearrange("p (a b) -> p a b", b=4)

    def bc(ap, n, w):
        return ap.unsqueeze(2).broadcast_to([128, n, w])

    def rv(fn, reads=(), writes=()):
        B.step("vector", fn, reads=list(reads) + [b_r, b_sctm, b_const],
               writes=list(writes) + [b_r])

    rv(lambda v: v.tensor_add(
        v8(sfc), v8(sc_tm),
        c_p["cbbc"][:, :].unsqueeze(1).broadcast_to([128, 16, 8])))

    def top1_mask(src_view, io_view, iom_view, n, w, out_mask_view):
        rv(lambda v: v.reduce_max(m1[:, 0:n], src_view, axis=AX.X))
        rv(lambda v: v.tensor_tensor(out_mask_view, src_view,
                                     bc(m1[:, 0:n], n, w), ALU.is_ge))
        rv(lambda v: v.tensor_tensor(r2[:, 0:n * w].rearrange(
            "p (a b) -> p a b", b=w), out_mask_view, iom_view, ALU.mult))
        rv(lambda v: v.tensor_scalar(r2[:, 0:n * w], r2[:, 0:n * w],
                                     BIGI, None, ALU.add))
        rv(lambda v: v.tensor_reduce(i1[:, 0:n], r2[:, 0:n * w].rearrange(
            "p (a b) -> p a b", b=w), axis=AX.X, op=ALU.min))
        rv(lambda v: v.tensor_tensor(out_mask_view, io_view,
                                     bc(i1[:, 0:n], n, w), ALU.is_equal))

    for gi, gdst in ((0, gs0), (1, gs1)):
        gview = v8(sfc)[:, :, gi * 4:(gi + 1) * 4]
        iov = v8(c_iota8[:, 0:128])[:, :, gi * 4:(gi + 1) * 4]
        iomv = v8(c_iotam[:, 0:128])[:, :, gi * 4:(gi + 1) * 4]
        f1v = v4(r1[:, 0:64])
        top1_mask(gview, iov, iomv, 16, 4, f1v)
        rv(lambda v, gview=gview, f1v=f1v: v.scalar_tensor_tensor(
            v4(r2[:, 0:64]), f1v, -BIGM, gview, ALU.mult, ALU.add))
        rv(lambda v: v.reduce_max(i1[:, 0:16], v4(r2[:, 0:64]), axis=AX.X))
        rv(lambda v, gdst=gdst: v.tensor_add(gdst[:, 0:16], m1[:, 0:16],
                                             i1[:, 0:16]))
    rv(lambda v: v.tensor_tensor(gs0[:, 0:16], gs0[:, 0:16], gs1[:, 0:16],
                                 ALU.is_ge))
    rv(lambda v: v.tensor_tensor(
        v8(sfc)[:, :, 0:4], v8(sfc)[:, :, 0:4], bc(gs0[:, 0:16], 16, 4),
        ALU.mult))
    rv(lambda v: v.tensor_scalar(gs1[:, 0:16], gs0[:, 0:16], -1.0, 1.0,
                                 ALU.mult, ALU.add))
    rv(lambda v: v.tensor_tensor(
        v8(sfc)[:, :, 4:8], v8(sfc)[:, :, 4:8], bc(gs1[:, 0:16], 16, 4),
        ALU.mult))
    f1v = v8(r1)
    top1_mask(v8(sfc), v8(c_iota8[:, 0:128]), v8(c_iotam[:, 0:128]),
              16, 8, f1v)
    rv(lambda v: v.scalar_tensor_tensor(v8(r2), f1v, -BIGM, v8(sfc),
                                        ALU.mult, ALU.add))
    f2v = v8(wts_tm)
    top1_mask(v8(r2), v8(c_iota8[:, 0:128]), v8(c_iotam[:, 0:128]),
              16, 8, f2v)
    rv(lambda v: v.tensor_add(v8(wts_tm), v8(wts_tm), f1v))
    rv(lambda v: v.tensor_mul(v8(wts_tm), v8(wts_tm), v8(sc_tm)))
    rv(lambda v: v.reduce_sum(dnm[:, 0:16], v8(wts_tm), axis=AX.X))
    rv(lambda v: v.tensor_scalar(dnm[:, 0:16], dnm[:, 0:16], 1e-20, None,
                                 ALU.add))
    B.step("scalar", lambda s: s.activation(dnm[:, 0:16], dnm[:, 0:16],
                                            AF.Ln),
           reads=[b_r], writes=[b_r])
    B.step("scalar", lambda s: s.activation(dnm[:, 0:16], dnm[:, 0:16],
                                            AF.Exp, scale=-1.0),
           reads=[b_r], writes=[b_r])
    wts_em = sfc.rearrange("p (e a) -> p a e", a=16)
    rv(lambda v: v.tensor_tensor(wts_em, v8(wts_tm),
                                 bc(dnm[:, 0:16], 16, 8), ALU.mult),
       writes=[b_wts])
    wEb = P4.bitcast(BF16)[:, 9632:9760]
    wcmT = P4.bitcast(BF16)[:, 9760:9888]
    B.step("vector", lambda v: v.tensor_copy(wEb[:, 0:128], sfc),
           reads=[b_wts], writes=[b_wts])
    B.step("tensor", lambda t: t.transpose(
        pb[1].bitcast(BF16)[:, 0:128], wEb[:, 0:128], c_identb[:]),
        reads=[b_wts, b_const], writes=[pbuf[1]])
    B.step("scalar", lambda s: s.copy(wcmT[:, 0:128],
                                      pb[1].bitcast(BF16)[:, 0:128]),
        reads=[pbuf[1]], writes=[b_wts])
    if "wts" in taps:
        dma("sync", tap["wts"], sfc, Buf("tapwts"), rbufs=[b_wts])

    mark('f1_router')
    b_lat = Buf("lat").retag(b_ro)
    lat_l = [P2[:, l * 2048:(l + 1) * 2048] for l in range(4)]
    for l in range(4):
        for tb in range(TBN):
            bk = 2 + (l * TBN + tb) % 2

            def mmf1(t, l=l, tb=tb, bk=bk):
                last = None
                for c in range(NCH):
                    last = t.matmul(
                        pb[bk][:],
                        fc1_ap[:, c * 512 + l * 128:c * 512 + (l + 1) * 128],
                        y2_c[c][:, tb * 512:(tb + 1) * 512],
                        start=(c == 0), stop=(c == NCH - 1))
                return last
            B.step("tensor", mmf1, reads=[b_fc1, b_y2], writes=[pbuf[bk]])
            B.step("scalar", lambda s, l=l, tb=tb, bk=bk: s.copy(
                lat_l[l][:, tb * 512:(tb + 1) * 512], pb[bk][:]),
                reads=[pbuf[bk]], writes=[b_lat])
    if "lat" in taps:
        for l in range(4):
            dma("sync", tap["lat"][l], lat_l[l], Buf(f"taplat{l}"),
                rbufs=[b_lat])

    mark('f1_fc1')
    b_comb = Buf("comb").retag(b_sc, *b_xP[0], *b_xP[1])
    comb_l = [P3[:, l * 2048:(l + 1) * 2048] for l in range(4)]
    b_h1w = Buf("h1w").retag(b_lat)
    h1w_i = [P2[:, 8192 + i * 2048:8192 + (i + 1) * 2048] for i in range(4)]
    P4b = P4.bitcast(BF16)
    b_wrowd = [Buf("wrowd0").retag(b_xq, b_r),
               Buf("wrowd1").retag(b_xq, b_r)]
    b_wbc = [Buf("wbc0b").retag(b_xq, b_r),
             Buf("wbc1b").retag(b_xq, b_r)]
    wrow_ap2 = [P4b[:, 0:2048], P4b[:, 2048:4096]]
    wbc_ap2 = [P4b[:, 4096:6144], P4b[:, 6144:8192]]
    tmps_ap = P3[:, 12288:16384]
    tmpA = [tmps_ap[:, j * 512:(j + 1) * 512] for j in range(4)]
    tmpB = [tmps_ap[:, 2048 + j * 512:2048 + (j + 1) * 512]
            for j in range(4)]
    _tmprt = [b_sc] + [b for tb in range(TBN) for b in (b_xP[2][tb],
                                                        b_xP[3][tb])]
    b_tmpA = [Buf(f"tmpA{j}").retag(*_tmprt) for j in range(4)]
    b_tmpB = [Buf(f"tmpB{j}").retag(*_tmprt) for j in range(4)]

    for e in range(NE):
        si = e % 2
        dma("sync", wrow_ap2[si][0:1, 0:2048],
            wcmT[16 * e:16 * e + 16, 0:128], b_wrowd[si], rbufs=[b_wts])
        for tb in range(TBN):
            def mmb(t, tb=tb, si=si):
                return t.matmul(pb[1][:], c_onesrowb[:],
                                wrow_ap2[si][0:1, tb * 512:(tb + 1) * 512],
                                start=True, stop=True)
            B.step("tensor", mmb, reads=[b_wrowd[si], b_const],
                   writes=[pbuf[1]])
            B.step("scalar", lambda s, tb=tb, si=si: s.copy(
                wbc_ap2[si][:, tb * 512:(tb + 1) * 512], pb[1][:]),
                reads=[pbuf[1]], writes=[b_wbc[si]])
        if e == 0 and "w1sb" in taps:
            dma("sync", tap["w1sb"], w1_ap[0], Buf("tapw1"),
                rbufs=[w1_sb[0]])
        if e == 0 and "wbc0" in taps:
            dma("sync", tap["wbc0"], wbc_ap2[0], Buf("tapwbc0"),
                rbufs=[b_wbc[0]])
        for i in range(4):
            for tb in range(TBN):
                bk = 4 + (i * TBN + tb) % 2

                def mmh(t, i=i, tb=tb, bk=bk, si=si):
                    last = None
                    for l in range(4):
                        last = t.matmul(
                            pb[bk][:],
                            w1_ap[si][:, l * 512 + i * 128:
                                      l * 512 + (i + 1) * 128],
                            lat_l[l][:, tb * 512:(tb + 1) * 512],
                            start=(l == 0), stop=(l == 3))
                    return last
                B.step("tensor", mmh, reads=[w1_sb[si], b_lat],
                       writes=[pbuf[bk]])
                j = (i * TBN + tb) % 4
                B.step("scalar", lambda s, bk=bk, j=j: s.activation(
                    tmpA[j], pb[bk][:], AF.Relu),
                    reads=[pbuf[bk]], writes=[b_tmpA[j]])
                if e == 0 and i == 0 and tb == 0 and "h1r00" in taps:
                    dma("sync", tap["h1r00"], tmpA[0], Buf("taph1r"),
                        rbufs=[b_tmpA[0]])
                B.step("vector", lambda v, j=j: v.tensor_mul(
                    tmpB[j], tmpA[j], tmpA[j]),
                    reads=[b_tmpA[j]], writes=[b_tmpB[j]])
                if e == 0 and i == 0 and tb == 0 and "h1r00" in taps:
                    dma("sync", tap["h1s00"], tmpB[0], Buf("taph1s"),
                        rbufs=[b_tmpB[0]])
                B.step("vector", lambda v, i=i, tb=tb, j=j, si=si:
                       v.tensor_mul(
                           h1w_i[i][:, tb * 512:(tb + 1) * 512], tmpB[j],
                           wbc_ap2[si][:, tb * 512:(tb + 1) * 512]),
                       reads=[b_tmpB[j], b_wbc[si]], writes=[b_h1w])
        if e == 0 and "h1w0" in taps:
            for i in range(4):
                dma("sync", tap["h1w0"][i], h1w_i[i], Buf(f"taph1w{i}"),
                    rbufs=[b_h1w])
        for l in range(4):
            for tb in range(TBN):
                bk = 6 + (l * TBN + tb) % 2

                def mmc(t, l=l, tb=tb, bk=bk, si=si):
                    last = None
                    for i in range(4):
                        last = t.matmul(
                            pb[bk][:],
                            w2_ap[si][:, i * 512 + l * 128:
                                      i * 512 + (l + 1) * 128],
                            h1w_i[i][:, tb * 512:(tb + 1) * 512],
                            start=(i == 0), stop=(i == 3))
                    return last
                B.step("tensor", mmc, reads=[w2_sb[si], b_h1w],
                       writes=[pbuf[bk]])
                if e == 0:
                    B.step("scalar", lambda s, l=l, tb=tb, bk=bk: s.copy(
                        comb_l[l][:, tb * 512:(tb + 1) * 512], pb[bk][:]),
                        reads=[pbuf[bk]], writes=[b_comb])
                else:
                    B.step("vector", lambda v, l=l, tb=tb, bk=bk:
                           v.tensor_add(
                               comb_l[l][:, tb * 512:(tb + 1) * 512],
                               comb_l[l][:, tb * 512:(tb + 1) * 512],
                               pb[bk][:]),
                           reads=[pbuf[bk], b_comb], writes=[b_comb])
        if e + 2 < NE:
            load_expert(e + 2)
    if "comb" in taps:
        for l in range(4):
            dma("sync", tap["comb"][l], comb_l[l], Buf(f"tapcomb{l}"),
                rbufs=[b_comb])

    mark('f1_experts')
    upg_ap = [PWb[:, 8256:12352], PWb[:, 12352:16448]]
    b_upg = [Buf("upg0").retag(w1_sb[0], w2_sb[0]),
             Buf("upg1").retag(w1_sb[1], w2_sb[1])]
    dng_ap = [PWb[:, 0:4096], P2[:, 8192:12288]]
    b_dng = [Buf("dng0").retag(b_gw, b_fc1),
             Buf("dng1").retag(b_h1w)]

    b_upq = Buf("upq").retag(b_ekv,
                             *[b for c in range(4, 8) for b in b_xP[c]])
    upq_ds = [P5[:, ds * 512:(ds + 1) * 512] for ds in range(32)]
    b_fin = Buf("fin").retag(b_scanA, b_scanB, b_xq, b_wrowd[0],
                             b_wrowd[1], b_wbc[0], b_wbc[1], b_xscr2)
    fin_o = [P4[:, o * 512:(o + 1) * 512] for o in range(8)]
    b_x1q = Buf("x1q").retag(b_sctm, b_wts, b_r, b_rt)
    x1q_ap = [P4[:, 4096:4608], P4[:, 4608:5120]]
    b_ost2 = [Buf("ostage0").retag(b_stat), Buf("ostage1").retag(b_stat)]
    ost_ap2 = [P4[:, 5120:6144], P4[:, 6144:7168]]
    b_tmp3 = Buf("f2tmp").retag(*b_tmpA, *b_tmpB)
    tmp3 = sqt2_ap[:, 1024:1536]

    upg_src = upTb_d.rearrange("g d p l e -> g p d (l e)")
    dng_src = downTb_d.rearrange("o sc p e -> o p sc e")

    last_out_tok = None
    for tb in range(TBN):
        for g in range(8):
            gb = g % 2
            dma("sync",
                upg_ap[gb].rearrange("p (d le) -> p d le", le=512),
                upg_src[g], b_upg[gb], rbufs=[b_upTb])
            for l in range(4):
                ds = g * 4 + l
                bk = (ds % 2)

                def mmu(t, g=g, l=l, tb=tb, bk=bk, gb=gb):
                    last = None
                    for dd in range(8):
                        last = t.matmul(
                            pb[bk][:],
                            upg_ap[gb][:, (dd * 4 + l) * 128:
                                       (dd * 4 + l + 1) * 128],
                            y2_c[dd][:, tb * 512:(tb + 1) * 512],
                            start=(dd == 0), stop=(dd == 7))
                    return last
                B.step("tensor", mmu, reads=[b_upg[gb], b_y2],
                       writes=[pbuf[bk]])
                B.step("scalar", lambda s, bk=bk: s.activation(
                    tmp3, pb[bk][:], AF.Relu),
                    reads=[pbuf[bk]], writes=[b_tmp3])
                B.step("vector", lambda v, ds=ds: v.tensor_mul(
                    upq_ds[ds], tmp3, tmp3),
                    reads=[b_tmp3], writes=[b_upq])
        for o in range(8):
            ob = o % 2
            dma("sync",
                dng_ap[ob].rearrange("p (sc e) -> p sc e", e=128),
                dng_src[o], b_dng[ob], rbufs=[b_dnTb])
            bk = 2 + o % 2

            def mmd(t, o=o, tb=tb, bk=bk, ob=ob):
                last = None
                for l in range(4):
                    last = t.matmul(
                        pb[bk][:],
                        fc2_ap[:, l * 1024 + o * 128:
                               l * 1024 + (o + 1) * 128],
                        comb_l[l][:, tb * 512:(tb + 1) * 512],
                        start=(l == 0), stop=False)
                for sc in range(32):
                    last = t.matmul(
                        pb[bk][:],
                        dng_ap[ob][:, sc * 128:(sc + 1) * 128],
                        upq_ds[sc], start=False, stop=(sc == 31))
                return last
            B.step("tensor", mmd,
                   reads=[b_fc2, b_comb, b_dng[ob], b_upq],
                   writes=[pbuf[bk]])
            xi = o % 2
            dma("scalar", x1q_ap[xi],
                x1_rows[o][:, tb * 512:(tb + 1) * 512], b_x1q,
                rbufs=[b_x1])
            B.step("vector", lambda v, o=o, xi=xi, bk=bk: v.tensor_add(
                fin_o[o], x1q_ap[xi], pb[bk][:]),
                reads=[b_x1q, pbuf[bk]], writes=[b_fin])
        for ti in range(4):
            ob2 = ti % 2
            bk0 = 4 + 2 * ob2
            for o in range(8):
                bk = bk0 + (o // 4)

                def emt2(t, o=o, ti=ti, bk=bk):
                    return t.transpose(
                        pb[bk][:, (o % 4) * 128:(o % 4 + 1) * 128],
                        fin_o[o][:, ti * 128:(ti + 1) * 128],
                        c_identf[:])
                B.step("tensor", emt2, reads=[b_fin, b_const],
                       writes=[pbuf[bk]])
            B.step("scalar", lambda s, ob2=ob2, bk0=bk0: s.copy(
                ost_ap2[ob2][:, 0:512], pb[bk0][:]),
                reads=[pbuf[bk0]], writes=[b_ost2[ob2]])
            B.step("scalar", lambda s, ob2=ob2, bk0=bk0: s.copy(
                ost_ap2[ob2][:, 512:1024], pb[bk0 + 1][:]),
                reads=[pbuf[bk0 + 1]], writes=[b_ost2[ob2]])
            t0 = tb * 512 + ti * 128
            last_out_tok = dma("sync", out[t0:t0 + 128, :], ost_ap2[ob2],
                               Buf(f"outw{tb}_{ti}"), rbufs=[b_ost2[ob2]])

    mark('f2_shared')
    fin_sync = B.dcnt["sync"] * 16
    fin_scalar = B.dcnt["scalar"] * 16
    fin_gpsimd = B.dcnt["gpsimd"] * 16

    def final_waits(eng):
        eng.wait_ge(B.semobj["d_sync"], fin_sync)
        eng.wait_ge(B.semobj["d_scalar"], fin_scalar)
        eng.wait_ge(B.semobj["d_gpsimd"], fin_gpsimd)
    B.raw("sync", final_waits)

    with nc.Block() as block:
        @block.sync
        def _(eng):
            for fn in B.q["sync"]:
                fn(eng)

        @block.tensor
        def _(eng):
            for fn in B.q["tensor"]:
                fn(eng)

        @block.vector
        def _(eng):
            for fn in B.q["vector"]:
                fn(eng)

        @block.scalar
        def _(eng):
            for fn in B.q["scalar"]:
                fn(eng)

        @block.gpsimd
        def _(eng):
            for fn in B.q["gpsimd"]:
                fn(eng)

    return nc


def _prep_inputs(ins):
    f = np.ascontiguousarray
    w = {}
    w["wkT"] = f(np.asarray(ins["L0_attn_Wk"]).T)
    w["wvT"] = f(np.asarray(ins["L0_attn_Wv"]).T)
    w["wrT"] = f(np.asarray(ins["L0_attn_Wr"]).T)
    w["woT"] = f(np.asarray(ins["L0_attn_Wo"]).T)
    w["gateT"] = f(np.asarray(ins["L0_ffn_gate"]).T)
    w["fc1T"] = f(np.asarray(ins["L0_ffn_fc1_w"]).T)
    w["fc2T"] = f(np.asarray(ins["L0_ffn_fc2_w"]).T)
    w["upT"] = f(np.asarray(ins["L0_ffn_up_w"]).T)
    dw = np.asarray(ins["L0_ffn_down_w"])
    w["downTr"] = f(dw.reshape(8, 128, DS).transpose(0, 2, 1))
    w["w1T"] = f(np.asarray(ins["L0_ffn_w1"]).transpose(0, 2, 1))
    w["w2T"] = f(np.asarray(ins["L0_ffn_w2"]).transpose(0, 2, 1))
    for nm, key in (("tdcm", "L0_attn_time_decay"),
                    ("tfcm", "L0_attn_time_first")):
        w[nm] = f(np.asarray(key and ins[key]).reshape(8, 128).T)
    for nm, key in (("tmkcm", "L0_attn_time_mix_k"),
                    ("tmvcm", "L0_attn_time_mix_v"),
                    ("tmrcm", "L0_attn_time_mix_r")):
        w[nm] = f(np.asarray(ins[key]).reshape(8, 128).T)
    cb = np.asarray(ins["L0_ffn_corr_bias"]).astype(np.float32)
    w["cbbc"] = f(np.broadcast_to(cb[None, :], (128, 8)))
    w["identf"] = np.eye(128, dtype=np.float32)
    w["identb"] = np.eye(128, dtype=ml_dtypes.bfloat16)
    w["ones128b"] = np.ones((128, 128), dtype=ml_dtypes.bfloat16)
    w["onesrowb"] = np.ones((1, 128), dtype=ml_dtypes.bfloat16)
    io = np.tile(np.arange(8, dtype=np.float32), 16)[None, :]
    w["iota8"] = f(np.broadcast_to(io, (128, 128)))
    w["iotam"] = f(np.broadcast_to(io - BIGI, (128, 128)))
    w["epscol"] = np.full((128, 1), 1e-5, dtype=np.float32)
    return w


def run(inputs, taps=(), cores=8):
    nc = build_program(taps)
    shared = _prep_inputs(inputs)
    xfull = np.asarray(inputs["L0_attn_x"], dtype=np.float32)
    in_maps = []
    for b in range(cores):
        m = dict(shared)
        m["x"] = np.ascontiguousarray(xfull[b])
        in_maps.append(m)
    res = run_bass_kernel_spmd(nc, in_maps, core_ids=list(range(cores)))
    return res


def kernel(**inputs):
    res = run(inputs, taps=(), cores=8)
    return np.stack([r["out"] for r in res.results]).astype(np.float32)


# version 09
# speedup vs baseline: 1.1023x; 1.0550x over previous
import numpy as np
import ml_dtypes
import concourse.bass as bass
import concourse.mybir as mybir
from concourse.bass_utils import run_bass_kernel_spmd

F32 = mybir.dt.float32
BF16 = mybir.dt.bfloat16
AF = mybir.ActivationFunctionType
ALU = mybir.AluOpType
AX = mybir.AxisListType

S = 2048
D = 1024
NCH = 8
TBN = 4
DL = 512
DS = 4096
NE = 8
BIGI = 1.0e6
BIGM = 1.0e30

ENGS = ("sync", "tensor", "vector", "scalar", "gpsimd")
DMA_ENGS = ("sync", "scalar", "gpsimd")


class Buf:
    __slots__ = ("name", "writer", "readers")

    def __init__(self, name):
        self.name = name
        self.writer = None
        self.readers = []

    def retag(self, *olds):
        for o in olds:
            if o.writer is not None:
                self.readers.append(o.writer)
            self.readers.extend(o.readers)
        return self


class Bld:
    def __init__(self, nc):
        self.nc = nc
        self.q = {e: [] for e in ENGS}
        self.semobj = {}
        self.ccnt = {e: 0 for e in ENGS}
        self.dcnt = {e: 0 for e in DMA_ENGS}
        for e in ENGS:
            self.semobj["c_" + e] = nc.alloc_semaphore("c_" + e)
        for e in DMA_ENGS:
            self.semobj["d_" + e] = nc.alloc_semaphore("d_" + e)
        self.waited = {e: {} for e in ENGS}
        self.log = []

    def step(self, e, emit, reads=(), writes=(), dma=False):
        deps = {}
        for blist, isw in ((reads, False), (writes, True)):
            for b in blist:
                if b.writer is not None:
                    k, v = b.writer
                    if deps.get(k, -1) < v:
                        deps[k] = v
                if isw:
                    for k, v in b.readers:
                        if deps.get(k, -1) < v:
                            deps[k] = v
        if dma:
            key = "d_" + e
            self.dcnt[e] += 1
            tok = (key, self.dcnt[e] * 16)
            incv = 16
        else:
            key = "c_" + e
            self.ccnt[e] += 1
            tok = (key, self.ccnt[e])
            incv = 1
        w = self.waited[e]
        waits = []
        for k, v in sorted(deps.items()):
            if w.get(k, 0) < v:
                waits.append((k, v))
                w[k] = v
        sem = self.semobj[key]

        def closure(eng, waits=waits, emit=emit, sem=sem, incv=incv):
            for k, v in waits:
                eng.wait_ge(self.semobj[k], v)
            last = emit(eng)
            last.then_inc(sem, incv)
        self.q[e].append(closure)
        if self.log is not None:
            self.log.append((e, tok, tuple(waits),
                             tuple(b.name for b in reads),
                             tuple(b.name for b in writes)))
        for b in reads:
            b.readers.append(tok)
        for b in writes:
            b.writer = tok
            b.readers = []
        return tok

    def raw(self, e, fn):
        self.q[e].append(fn)


PHASE_MARKS = {}
LAST_B = [None]


def build_program(taps=()):
    nc = bass.Bass("TRN2", target_bir_lowering=False, debug=False)
    B = Bld(nc)
    LAST_B[0] = B

    def din(name, shape, dt=F32):
        return nc.dram_tensor(name, shape, dt, kind="ExternalInput").ap()

    def dout(name, shape, dt=F32):
        return nc.dram_tensor(name, shape, dt, kind="ExternalOutput").ap()

    x = din("x", [S, D])
    wT = {n: din(n, [D, D]) for n in ("wkT", "wvT", "wrT", "woT")}
    gateT = din("gateT", [D, NE])
    fc1T = din("fc1T", [D, DL])
    fc2T = din("fc2T", [DL, D])
    upT = din("upT", [D, DS])
    downTr = din("downTr", [8, DS, 128])
    w1T = din("w1T", [NE, DL, DL])
    w2T = din("w2T", [NE, DL, DL])
    prm = {n: din(n, [128, 8]) for n in
           ("tdcm", "tfcm", "tmkcm", "tmvcm", "tmrcm", "cbbc")}
    identf = din("identf", [128, 128])
    identb = din("identb", [128, 128], BF16)
    ones128b = din("ones128b", [128, 128], BF16)
    onesrowb = din("onesrowb", [1, 128], BF16)
    iota8 = din("iota8", [128, 128])
    iotam = din("iotam", [128, 128])
    epscol = din("epscol", [128, 1])

    out = dout("out", [S, D])
    tap = {}
    if "xn" in taps:
        tap["xn"] = dout("t_xn", [NCH, 128, 2052], BF16)
    if "ek" in taps:
        tap["ek"] = dout("t_ek", [NCH, 128, S], BF16)
    if "wkv" in taps:
        tap["wkv"] = dout("t_wkv", [NCH, 128, S], BF16)
    if "ekv" in taps:
        tap["ekv"] = dout("t_ekv", [NCH, 128, S], BF16)
    if "scan7" in taps:
        tap["scan7A"] = dout("t_scan7A", [128, 2052])
        tap["scan7B"] = dout("t_scan7B", [128, 2052])
    if "nd7" in taps:
        tap["num7"] = dout("t_num7", [128, S], BF16)
        tap["den7"] = dout("t_den7", [128, S], BF16)
        tap["rden7"] = dout("t_rden7", [128, S])
    if "ro" in taps:
        tap["ro"] = dout("t_ro", [NCH, 128, S], BF16)
    if "x1" in taps:
        tap["x1"] = dout("t_x1", [D, S])
    if "xcm" in taps:
        tap["xcm"] = dout("t_xcm", [D, S])
    if "y2" in taps:
        tap["y2"] = dout("t_y2", [NCH, 128, S], BF16)
    if "wts" in taps:
        tap["wts"] = dout("t_wts", [128, 128])
    if "sctm" in taps:
        tap["sctm"] = dout("t_sctm", [128, 128])
    if "lat" in taps:
        tap["lat"] = dout("t_lat", [4, 128, S], BF16)
    if "comb" in taps:
        tap["comb"] = dout("t_comb", [4, 128, S], BF16)
    if "wbc0" in taps:
        tap["wbc0"] = dout("t_wbc0", [128, S], BF16)
    if "h1w0" in taps:
        tap["h1w0"] = dout("t_h1w0", [4, 128, S], BF16)
    if "he0" in taps:
        tap["he0"] = dout("t_he0", [4, 128, S], BF16)
    if "h1r00" in taps:
        tap["h1r00"] = dout("t_h1r00", [128, 512], BF16)
        tap["h1s00"] = dout("t_h1s00", [128, 512], BF16)
    if "w1sb" in taps:
        tap["w1sb"] = dout("t_w1sb", [128, 2048], BF16)

    xcm_d = nc.dram_tensor("xcm_d", [D, S], F32).ap()
    x1_d = nc.dram_tensor("x1_d", [D, S], F32).ap()
    upTb_d = nc.dram_tensor("upTb_d", [8, 8, 128, 4, 128], BF16).ap()
    downTb_d = nc.dram_tensor("downTb_d", [8, 32, 128, 128], BF16).ap()

    sb = nc.alloc_sbuf_tensor
    c_identf = sb("c_identf", [128, 128], F32)
    c_ones128b = sb("c_ones128b", [128, 128], BF16)
    c_onesrowb = sb("c_onesrowb", [1, 128], BF16)
    c_identb = sb("c_identb", [128, 128], BF16)
    c_iota8 = sb("c_iota8", [128, 128], F32)
    c_iotam = sb("c_iotam", [128, 128], F32)
    c_eps = sb("c_eps", [128, 1], F32)
    c_p = {n: sb("c_" + n, [128, 8], F32) for n in prm}
    c_ew = sb("c_ew", [128, 8], F32)
    c_eu = sb("c_eu", [128, 8], F32)

    P1 = sb("P1", [128, 16384], BF16)
    P2 = sb("P2", [128, 16416], BF16)
    P3 = sb("P3", [128, 16384], BF16)
    P5 = sb("P5", [128, 16384], BF16)
    P4 = sb("P4", [128, 8192], F32)
    PW = sb("PW", [128, 16896], BF16)
    PWb = PW.bitcast(BF16)

    pb = [nc.alloc_psum_tensor(f"pbank{i}", [128, 512], F32)
          for i in range(8)]
    pbuf = [Buf(f"pb{i}") for i in range(8)]

    b_const = Buf("const")

    def dma(e, dst, src, wbuf, rbufs=(), **kw):
        return B.step(e, lambda eng: eng.dma_start(out=dst, in_=src, **kw),
                      reads=list(rbufs), writes=[wbuf], dma=True)

    for dst, src in ((c_identf[:], identf), (c_identb[:], identb),
                     (c_ones128b[:], ones128b), (c_onesrowb[:], onesrowb),
                     (c_iota8[:], iota8), (c_iotam[:], iotam),
                     (c_eps[:], epscol)):
        dma("scalar", dst, src, b_const)
    for n in prm:
        dma("scalar", c_p[n][:], prm[n], b_const)
    B.step("scalar", lambda s: s.activation(c_eu[:], c_p["tdcm"][:], AF.Exp),
           reads=[b_const], writes=[b_const])
    B.step("scalar", lambda s: s.activation(c_ew[:], c_eu[:], AF.Exp,
                                            scale=-1.0),
           reads=[b_const], writes=[b_const])
    B.step("scalar", lambda s: s.activation(c_eu[:], c_p["tfcm"][:], AF.Exp),
           reads=[b_const], writes=[b_const])

    w_slot = [PWb[:, 0:8192], PWb[:, 8256:16448]]
    w_sbuf = [Buf("wslot0"), Buf("wslot1")]

    def load_wT(slot, name):
        src = wT[name].rearrange("(c p) e -> p c e", p=128)
        dst = w_slot[slot].rearrange("p (c e) -> p c e", e=1024)
        return dma("gpsimd", dst, src, w_sbuf[slot])

    load_wT(0, "wkT")
    load_wT(1, "wvT")

    upsrc = upT.rearrange("(d p) (g l e) -> g p d (l e)", p=128, g=8,
                          e=128)
    updst = upTb_d.rearrange("g d p l e -> g p d (l e)")
    dnsrc = downTr.rearrange("o (sc p) e -> o p sc e", p=128)
    dndst = downTb_d.rearrange("o sc p e -> o p sc e")
    b_upTb = Buf("upTb")
    b_dnTb = Buf("dnTb")

    def emit_casts(items):
        for kind, idx in items:
            if kind == "u":
                dma("gpsimd", updst[idx], upsrc[idx], b_upTb)
            else:
                dma("gpsimd", dndst[idx], dnsrc[idx], b_dnTb)


    PHASE_MARKS.clear()
    def mark(name):
        PHASE_MARKS[name] = {e: B.ccnt[e] + B.dcnt.get(e, 0) * 0
                             for e in ENGS}
        PHASE_MARKS[name] = {e: len(B.q[e]) for e in ENGS}
    mark('ph0_end')
    P1f = P1.bitcast(F32)
    P3f = P3.bitcast(F32)
    b_xcm_sb = [Buf(f"xcmsb{c}") for c in range(NCH)]
    b_xcm_cg = [[Buf(f"xcmsb{c}_{g}") for g in range(4)]
                for c in range(NCH)]

    def xcm_sb(c):
        pool = P1f if c < 4 else P3f
        return pool[:, (c % 4) * 2048:(c % 4 + 1) * 2048]

    b_xin = [Buf("xin0"), Buf("xin1")]
    xin_ap = [P4[:, 0:2048], P4[:, 2048:4096]]
    xcm_rows = xcm_d.rearrange("(c p) t -> c p t", p=128)
    xsrc = x.rearrange("(g i p) d -> g p i d", p=128, g=8)
    for g in range(8):
        bi = g % 2
        dma("sync", xin_ap[bi].rearrange("p (i d) -> p i d", d=1024),
            xsrc[g], b_xin[bi])
        for c in range(NCH):
            bk = c % 2

            def emtg(t, g=g, c=c, bi=bi, bk=bk):
                last = None
                for i in range(2):
                    last = t.transpose(
                        pb[bk][:, (g % 2) * 256 + i * 128:
                               (g % 2) * 256 + (i + 1) * 128],
                        xin_ap[bi][:, i * 1024 + c * 128:
                                   i * 1024 + (c + 1) * 128],
                        c_identf[:])
                return last
            B.step("tensor", emtg, reads=[b_xin[bi], b_const],
                   writes=[pbuf[bk]])
            B.step("scalar", lambda s, g=g, c=c, bk=bk: s.copy(
                xcm_sb(c)[:, g * 256:(g + 1) * 256],
                pb[bk][:, (g % 2) * 256:(g % 2) * 256 + 256]),
                reads=[pbuf[bk]], writes=[b_xcm_cg[c][g // 2]])

    b_xq = Buf("xq").retag(b_xin[0], b_xin[1])
    b_xbq = Buf("xbq")
    b_stat = Buf("stat")
    xq_ap = P4[:, 0:4096]
    xbq_ap = P4.bitcast(BF16)[:, 8192:12288]
    mbc_ap = P4[:, 6144:6656]
    rbc_ap = P4[:, 6656:7168]
    vtmp_ap = P4[:, 5632:6144]

    def emit_layernorm(src_d, b_src, dst_ap_fn, b_dst, b_sqt, sqt_ap,
                       b_xbq=b_xbq, sb_src=None, sb_bufs=None,
                       xscr=None, b_xscr=None, banks=(6, 7)):
        if sb_src is None:
            src = src_d.rearrange("(c p) t -> p c t", p=128)
        for tb in range(TBN):
            if sb_src is None:
                dma("sync", xq_ap.rearrange("p (c t) -> p c t", t=512),
                    src[:, :, tb * 512:(tb + 1) * 512],
                    b_xq, rbufs=[b_src])

            def srcap(c, tb=tb):
                if sb_src is None:
                    return xq_ap[:, c * 512:(c + 1) * 512]
                return sb_src(c)[:, tb * 512:(tb + 1) * 512]

            def srcbuf(c, tb=tb):
                if sb_src is None:
                    return b_xq
                sel = sb_bufs[c]
                return sel[tb] if isinstance(sel, list) else sel
            for c in range(NCH):
                B.step("vector", lambda v, c=c, ap=srcap(c): v.tensor_copy(
                    xbq_ap[:, c * 512:(c + 1) * 512], ap),
                    reads=[srcbuf(c)], writes=[b_xbq])
                B.step("scalar", lambda s, c=c, ap=srcap(c): s.activation(
                    sqt_ap[:, c * 512:(c + 1) * 512], ap, AF.Square),
                    reads=[srcbuf(c)], writes=[b_sqt])

            bks, bkq = banks

            def mm_s1(t):
                last = None
                for c in range(NCH):
                    last = t.matmul(pb[bks][:], c_ones128b[:],
                                    xbq_ap[:, c * 512:(c + 1) * 512],
                                    start=(c == 0), stop=(c == NCH - 1))
                return last
            B.step("tensor", mm_s1, reads=[b_xbq, b_const],
                   writes=[pbuf[bks]])

            def mm_s2(t):
                last = None
                for c in range(NCH):
                    last = t.matmul(pb[bkq][:], c_ones128b[:],
                                    sqt_ap[:, c * 512:(c + 1) * 512],
                                    start=(c == 0), stop=(c == NCH - 1))
                return last
            B.step("tensor", mm_s2, reads=[b_sqt, b_const],
                   writes=[pbuf[bkq]])
            B.step("vector", lambda v, bks=bks: v.tensor_scalar_mul(
                mbc_ap, pb[bks][:], 1.0 / D), reads=[pbuf[bks]],
                writes=[b_stat])
            B.step("vector", lambda v, bkq=bkq: v.tensor_scalar_mul(
                rbc_ap, pb[bkq][:], 1.0 / D), reads=[pbuf[bkq]],
                writes=[b_stat])
            B.step("vector", lambda v: v.tensor_mul(
                vtmp_ap, mbc_ap, mbc_ap), reads=[b_stat], writes=[b_stat])
            B.step("vector", lambda v: v.tensor_sub(
                rbc_ap, rbc_ap, vtmp_ap), reads=[b_stat], writes=[b_stat])
            B.step("scalar", lambda s: s.activation(
                vtmp_ap, rbc_ap, AF.Ln, bias=c_eps[:, 0:1]),
                reads=[b_stat], writes=[b_stat])
            B.step("scalar", lambda s: s.activation(
                rbc_ap, vtmp_ap, AF.Exp, scale=-0.5),
                reads=[b_stat], writes=[b_stat])
            for c in range(NCH):
                if sb_src is None:
                    B.step("vector", lambda v, c=c: v.tensor_sub(
                        xq_ap[:, c * 512:(c + 1) * 512],
                        xq_ap[:, c * 512:(c + 1) * 512], mbc_ap),
                        reads=[b_xq, b_stat], writes=[b_xq])
                    B.step("vector", lambda v, c=c, tb=tb: v.tensor_mul(
                        dst_ap_fn(c, tb),
                        xq_ap[:, c * 512:(c + 1) * 512], rbc_ap),
                        reads=[b_xq, b_stat], writes=[b_dst])
                else:
                    B.step("vector", lambda v, c=c, ap=srcap(c):
                           v.tensor_sub(xscr, ap, mbc_ap),
                           reads=[srcbuf(c), b_stat], writes=[b_xscr])
                    B.step("vector", lambda v, c=c, tb=tb: v.tensor_mul(
                        dst_ap_fn(c, tb), xscr, rbc_ap),
                        reads=[b_xscr, b_stat], writes=[b_dst])

    mark('a0_xpose')
    XNW = 2052
    b_xn = Buf("xn")
    xn_c = [P2[:, c * XNW:c * XNW + 2049] for c in range(NCH)]
    B.step("vector", lambda v: v.memset(P2[:, :], 0.0), writes=[b_xn])
    b_sqt1 = Buf("sqt1")
    sqt1_ap = P5[:, 8192:12288]

    b_xscr1 = Buf("xscr1")
    emit_layernorm(None, None,
                   lambda c, tb: xn_c[c][:, 1 + tb * 512:1 + (tb + 1) * 512],
                   b_xn, b_sqt1, sqt1_ap, sb_src=xcm_sb, sb_bufs=b_xcm_cg,
                   xscr=P4[:, 7168:7680], b_xscr=b_xscr1)
    spill_tok = None
    for c in range(NCH):
        spill_tok = dma("scalar", xcm_rows[c], xcm_sb(c),
                        Buf(f"spill{c}"), rbufs=b_xcm_cg[c])
    b_xcm = Buf("xcm_d")
    b_xcm.writer = spill_tok
    if "xcm" in taps:
        dma("sync", tap["xcm"], xcm_d, Buf("tapxcm"), rbufs=[b_xcm])
    for c in range(NCH):
        pass
    if "xn" in taps:
        for c in range(NCH):
            dma("sync", tap["xn"][c], P2[:, c * XNW:(c + 1) * XNW],
                Buf(f"tapxn{c}"), rbufs=[b_xn])

    mark('a1_ln1')
    b_mix = Buf("mix").retag(*[b for c in range(4) for b in b_xcm_cg[c]])
    mix_c = [P1[:, c * 2048:(c + 1) * 2048] for c in range(NCH)]
    b_ek = Buf("ek").retag(b_sqt1,
                           *[b for c in range(4, 8) for b in b_xcm_cg[c]])
    ek_c = [P3[:, c * 2048:(c + 1) * 2048] for c in range(NCH)]
    b_ekv = Buf("ekv").retag(b_sqt1)
    ekv_c = [P5[:, c * 2048:(c + 1) * 2048] for c in range(NCH)]
    b_sig = Buf("sig").retag(b_xn)
    sig_c = [P2[:, c * 2048:(c + 1) * 2048] for c in range(NCH)]

    b_mixtb = [Buf(f"mixtb{tb}").retag(b_mix) for tb in range(TBN)]

    def emit_mix_tb(tmcol, tb):
        for c in range(NCH):
            xncur = xn_c[c][:, 1 + tb * 512:1 + (tb + 1) * 512]
            xnsh = xn_c[c][:, tb * 512:(tb + 1) * 512]
            mslice = mix_c[c][:, tb * 512:(tb + 1) * 512]
            B.step("vector", lambda v, m=mslice, a=xncur, b=xnsh:
                   v.tensor_sub(m, a, b),
                   reads=[b_xn], writes=[b_mixtb[tb]])
            B.step("vector", lambda v, m=mslice, b=xnsh, c=c, tmcol=tmcol:
                   v.scalar_tensor_tensor(m, m, tmcol[:, c:c + 1], b,
                                          ALU.mult, ALU.add),
                   reads=[b_mixtb[tb], b_xn, b_const], writes=[b_mixtb[tb]])

    def emit_mix(tmcol):
        for tb in range(TBN):
            emit_mix_tb(tmcol, tb)

    def emit_proj(slot, post, banks):
        for tb in range(TBN):
            for e in range(NCH):
                bi = banks[(tb * NCH + e) % len(banks)]

                def mmf(t, e=e, tb=tb, bi=bi):
                    last = None
                    for c in range(NCH):
                        last = t.matmul(
                            pb[bi][:],
                            w_slot[slot][:, c * 1024 + e * 128:
                                         c * 1024 + (e + 1) * 128],
                            mix_c[c][:, tb * 512:(tb + 1) * 512],
                            start=(c == 0), stop=(c == NCH - 1))
                    return last
                B.step("tensor", mmf, reads=[w_sbuf[slot], b_mixtb[tb]],
                       writes=[pbuf[bi]])
                post(e, tb, bi)

    emit_mix(c_p["tmkcm"])

    def post_k(e, tb, bi):
        B.step("scalar", lambda s, e=e, tb=tb, bi=bi: s.activation(
            ek_c[e][:, tb * 512:(tb + 1) * 512], pb[bi][:], AF.Exp),
            reads=[pbuf[bi]], writes=[b_ek])
    emit_proj(0, post_k, (0, 1, 2, 3))
    if "ek" in taps:
        for c in range(NCH):
            dma("sync", tap["ek"][c], ek_c[c], Buf(f"tapek{c}"),
                rbufs=[b_ek])

    emit_mix(c_p["tmvcm"])
    load_wT(0, "wrT")
    emit_casts([("u", g) for g in range(8)])

    def post_v(e, tb, bi):
        B.step("vector", lambda v, e=e, tb=tb, bi=bi: v.tensor_mul(
            ekv_c[e][:, tb * 512:(tb + 1) * 512],
            ek_c[e][:, tb * 512:(tb + 1) * 512], pb[bi][:]),
            reads=[pbuf[bi], b_ek], writes=[b_ekv])
    emit_proj(1, post_v, (0, 1, 2, 3))

    emit_mix(c_p["tmrcm"])
    load_wT(1, "woT")
    emit_casts([("d", o) for o in range(8)])
    b_upTb.writer = ("d_gpsimd", B.dcnt["gpsimd"] * 16)
    b_dnTb.writer = ("d_gpsimd", B.dcnt["gpsimd"] * 16)
    if "ekv" in taps:
        for c in range(NCH):
            dma("sync", tap["ekv"][c], ekv_c[c], Buf(f"tapekv{c}"),
                rbufs=[b_ekv])
    b_scanA = Buf("scanA").retag(b_xq, b_xbq, b_xin[0], b_xscr1)
    b_scanB = Buf("scanB").retag(b_xq, b_xbq, b_xin[0], b_xscr1)
    scanA = P4[:, 0:2052]
    scanB = P4[:, 2052:4104]
    B.step("vector", lambda v: v.memset(scanA[:, 0:1], 0.0),
           writes=[b_scanA])
    B.step("vector", lambda v: v.memset(scanB[:, 0:1], 0.0),
           writes=[b_scanB])
    b_wkv = Buf("wkv")
    for c in range(NCH):
        ewc = c_ew[:, c:c + 1]
        euc = c_eu[:, c:c + 1]
        B.step("vector", lambda v, c=c, ewc=ewc: v.tensor_tensor_scan(
            scanA[:, 1:2049], ewc.broadcast_to([128, 2048]), ekv_c[c],
            0.0, ALU.mult, ALU.add),
            reads=[b_ekv, b_const], writes=[b_scanA])
        B.step("vector", lambda v, c=c, ewc=ewc: v.tensor_tensor_scan(
            scanB[:, 1:2049], ewc.broadcast_to([128, 2048]), ek_c[c],
            0.0, ALU.mult, ALU.add),
            reads=[b_ek, b_const], writes=[b_scanB])
        B.step("vector", lambda v, c=c, euc=euc: v.scalar_tensor_tensor(
            ekv_c[c], ekv_c[c], euc, scanA[:, 0:2048], ALU.mult, ALU.add),
            reads=[b_ekv, b_scanA, b_const], writes=[b_ekv])
        B.step("vector", lambda v, c=c, euc=euc: v.scalar_tensor_tensor(
            ek_c[c], ek_c[c], euc, scanB[:, 0:2048], ALU.mult, ALU.add),
            reads=[b_ek, b_scanB, b_const], writes=[b_ek])
        if c == 7 and "nd7" in taps:
            dma("sync", tap["num7"], ekv_c[c], Buf("tapnum7"),
                rbufs=[b_ekv])
            dma("sync", tap["den7"], ek_c[c], Buf("tapden7"),
                rbufs=[b_ek])
        if c == 7 and "scan7" in taps:
            dma("sync", tap["scan7A"], scanA, Buf("tapscan7A"),
                rbufs=[b_scanA])
            dma("sync", tap["scan7B"], scanB, Buf("tapscan7B"),
                rbufs=[b_scanB])
        B.step("scalar", lambda s, c=c: s.activation(
            scanB[:, 1:2049], ek_c[c], AF.Ln),
            reads=[b_ek], writes=[b_scanB])
        B.step("scalar", lambda s: s.activation(
            scanA[:, 1:2049], scanB[:, 1:2049], AF.Exp, scale=-1.0),
            reads=[b_scanB, b_scanA], writes=[b_scanA])
        B.step("scalar", lambda s: s.copy(
            scanB[:, 0:8], scanA[:, 2041:2049]),
            reads=[b_scanA], writes=[b_scanB])
        if c == 7 and "nd7" in taps:
            dma("sync", tap["rden7"], scanA[:, 0:2048], Buf("taprd7"),
                rbufs=[b_scanA])
        B.step("vector", lambda v, c=c: v.tensor_mul(
            ekv_c[c], ekv_c[c], scanA[:, 1:2049]),
            reads=[b_ekv, b_scanA, b_scanB], writes=[b_wkv])

    b_rt = Buf("rt").retag(b_xbq)
    rt_ap = P4[:, 4096:4608]
    rt2_ap = P4[:, 4608:5120]

    def post_r(e, tb, bi):
        B.step("scalar", lambda s, bi=bi: s.activation(
            rt_ap, pb[bi][:], AF.Exp, scale=-1.0),
            reads=[pbuf[bi]], writes=[b_rt])
        B.step("scalar", lambda s: s.activation(
            rt2_ap, rt_ap, AF.Ln, bias=1.0),
            reads=[b_rt], writes=[b_rt])
        B.step("scalar", lambda s, e=e, tb=tb: s.activation(
            sig_c[e][:, tb * 512:(tb + 1) * 512], rt2_ap, AF.Exp,
            scale=-1.0),
            reads=[b_rt], writes=[b_sig])
    emit_proj(0, post_r, (0, 1, 2, 3))


    mark('a4_proj')
    b_ro = Buf("ro").retag(b_sig)
    ro_c = sig_c
    for c in range(NCH):
        if "wkv" in taps:
            dma("sync", tap["wkv"][c], ekv_c[c], Buf(f"tapwkv{c}"),
                rbufs=[b_wkv])
        B.step("vector", lambda v, c=c: v.tensor_mul(
            ro_c[c], ekv_c[c], ro_c[c]),
            reads=[b_wkv, b_sig, b_ro], writes=[b_ro])
    if "ro" in taps:
        for c in range(NCH):
            dma("sync", tap["ro"][c], ro_c[c], Buf(f"tapro{c}"),
                rbufs=[b_ro])

    mark('a5_wkv')
    P5f = P5.bitcast(F32)
    b_xP = [[Buf(f"x1sb{c}_{tb}") for tb in range(TBN)]
            for c in range(NCH)]
    for c in range(NCH):
        rt = [b_ek] if c < 4 else [b_ekv, b_wkv, b_sqt1]
        for tb in range(TBN):
            b_xP[c][tb].retag(*rt)

    def xP(c):
        pool = P3f if c < 4 else P5f
        return pool[:, (c % 4) * 2048:(c % 4 + 1) * 2048]

    b_xld = [Buf(f"xld{o}") for o in range(NCH)]
    for o in range(NCH):
        b_xld[o].retag(*b_xP[o])
        dma("sync", xP(o), xcm_rows[o], b_xld[o], rbufs=[b_xcm])
    x1_rows = x1_d.rearrange("(c p) t -> c p t", p=128)
    b_x1 = Buf("x1_d")
    last_x1_tok = None
    for tb in range(TBN):
        for o in range(NCH):
            bk = 6 + o % 2

            def mmo(t, o=o, tb=tb, bk=bk):
                last = None
                for c in range(NCH):
                    last = t.matmul(
                        pb[bk][:],
                        w_slot[1][:, c * 1024 + o * 128:
                                  c * 1024 + (o + 1) * 128],
                        ro_c[c][:, tb * 512:(tb + 1) * 512],
                        start=(c == 0), stop=(c == NCH - 1))
                return last
            B.step("tensor", mmo, reads=[w_sbuf[1], b_ro], writes=[pbuf[bk]])
            B.step("vector", lambda v, o=o, tb=tb, bk=bk: v.tensor_add(
                xP(o)[:, tb * 512:(tb + 1) * 512],
                xP(o)[:, tb * 512:(tb + 1) * 512], pb[bk][:]),
                reads=[b_xld[o], pbuf[bk]], writes=[b_xP[o][tb]])
            if tb == TBN - 1:
                last_x1_tok = dma("scalar", x1_rows[o], xP(o),
                                  Buf(f"x1w{o}"), rbufs=b_xP[o])
    b_x1.writer = last_x1_tok

    if "x1" in taps:
        dma("sync", tap["x1"], x1_d, Buf("tapx1"), rbufs=[b_x1])

    mark('a6_wo')
    b_y2 = Buf("y2").retag(*b_mixtb)
    y2_c = [P1[:, c * 2048:(c + 1) * 2048] for c in range(NCH)]
    b_xbq2 = Buf("xbq2").retag(b_xbq, b_rt)
    b_sqt2 = Buf("sqt2").retag(b_ro)
    sqt2_ap = P2[:, 0:4096]
    b_xscr2 = Buf("xscr2").retag(b_xscr1, b_scanA)
    emit_layernorm(None, None,
                   lambda c, tb: y2_c[c][:, tb * 512:(tb + 1) * 512],
                   b_y2, b_sqt2, sqt2_ap, b_xbq=b_xbq2,
                   sb_src=xP, sb_bufs=b_xP,
                   xscr=P4[:, 7168:7680], b_xscr=b_xscr2, banks=(4, 5))
    if "y2" in taps:
        for c in range(NCH):
            dma("sync", tap["y2"][c], y2_c[c], Buf(f"tapy2{c}"),
                rbufs=[b_y2])

    b_gw = Buf("gatew").retag(w_sbuf[0])
    g_ap = PWb[:, 0:64]
    dma("gpsimd", g_ap.rearrange("p (c e) -> p c e", e=8),
        gateT.rearrange("(c p) e -> p c e", p=128), b_gw)
    b_fc1 = Buf("fc1w").retag(w_sbuf[0])
    fc1_ap = PWb[:, 64:4160]
    dma("gpsimd", fc1_ap.rearrange("p (c l) -> p c l", l=512),
        fc1T.rearrange("(c p) l -> p c l", p=128), b_fc1)
    b_fc2 = Buf("fc2w").retag(w_sbuf[0])
    fc2_ap = PWb[:, 4160:8256]
    dma("gpsimd", fc2_ap.rearrange("p (l d) -> p l d", d=1024),
        fc2T.rearrange("(l p) d -> p l d", p=128), b_fc2)
    w1_ap = [PWb[:, 8256:10304], PWb[:, 12352:14400]]
    w2_ap = [PWb[:, 10304:12352], PWb[:, 14400:16448]]
    w1_sb = [Buf("w1s0").retag(w_sbuf[1]), Buf("w1s1").retag(w_sbuf[1])]
    w2_sb = [Buf("w2s0").retag(w_sbuf[1]), Buf("w2s1").retag(w_sbuf[1])]
    w1src = w1T.rearrange("e (l p) i -> e p l i", p=128)
    w2src = w2T.rearrange("e (i p) l -> e p i l", p=128)

    def load_expert(e):
        si = e % 2
        dma("gpsimd", w1_ap[si].rearrange("p (l i) -> p l i", i=512),
            w1src[e], w1_sb[si])
        dma("gpsimd", w2_ap[si].rearrange("p (i l) -> p i l", l=512),
            w2src[e], w2_sb[si])
    load_expert(0)
    load_expert(1)

    sc_cm = P3.bitcast(F32)[:, 4096:6144]
    sct2 = P3.bitcast(F32)[:, 6144:6656]
    b_sc = Buf("sc").retag(b_ek, *b_xP[2], *b_xP[3])
    for tb in range(TBN):
        def mmg(t, tb=tb):
            last = None
            for c in range(NCH):
                last = t.matmul(pb[0][0:8, :], g_ap[:, c * 8:(c + 1) * 8],
                                y2_c[c][:, tb * 512:(tb + 1) * 512],
                                start=(c == 0), stop=(c == NCH - 1))
            return last
        B.step("tensor", mmg, reads=[b_gw, b_y2], writes=[pbuf[0]])
        B.step("scalar", lambda s, tb=tb: s.activation(
            sc_cm[0:8, tb * 512:(tb + 1) * 512], pb[0][0:8, :], AF.Exp,
            scale=-1.0), reads=[pbuf[0]], writes=[b_sc])
        B.step("scalar", lambda s, tb=tb: s.activation(
            sct2[0:8, 0:512],
            sc_cm[0:8, tb * 512:(tb + 1) * 512], AF.Ln, bias=1.0),
            reads=[b_sc], writes=[b_sc])
        B.step("scalar", lambda s, tb=tb: s.activation(
            sc_cm[0:8, tb * 512:(tb + 1) * 512],
            sct2[0:8, 0:512], AF.Exp, scale=-1.0),
            reads=[b_sc], writes=[b_sc])

    sc_tm = P4[:, 4096:4224]
    b_sctm = Buf("sc_tm").retag(b_rt, b_xbq2)
    for i in range(16):
        def emt(t, i=i):
            return t.transpose(pb[1][:, (i % 8) * 8:(i % 8) * 8 + 8],
                               sc_cm[0:8, i * 128:(i + 1) * 128],
                               c_identf[0:8, 0:8])
        B.step("tensor", emt, reads=[b_sc, b_const], writes=[pbuf[1]])
        if i % 8 == 7:
            B.step("scalar", lambda s, i=i: s.copy(
                sc_tm[:, (i - 7) * 8:(i + 1) * 8], pb[1][:, 0:64]),
                reads=[pbuf[1]], writes=[b_sctm])

    if "sctm" in taps:
        dma("sync", tap["sctm"], sc_tm, Buf("tapsctm"), rbufs=[b_sctm])
    wts_tm = P4[:, 4224:4352]
    b_wts = Buf("wts")
    sfc = P4[:, 4352:4480]
    r1 = P4[:, 4480:4608]
    r2 = P4[:, 4608:4736]
    m1 = P4[:, 4736:4752]
    i1 = P4[:, 4752:4768]
    gs0 = P4[:, 4768:4784]
    gs1 = P4[:, 4784:4800]
    dnm = P4[:, 4800:4816]
    b_r = Buf("router").retag(b_xq, b_xbq2)

    def v8(ap):
        return ap.rearrange("p (a b) -> p a b", b=8)

    def v4(ap):
        return ap.rearrange("p (a b) -> p a b", b=4)

    def bc(ap, n, w):
        return ap.unsqueeze(2).broadcast_to([128, n, w])

    def rv(fn, reads=(), writes=()):
        B.step("vector", fn, reads=list(reads) + [b_r, b_sctm, b_const],
               writes=list(writes) + [b_r])

    rv(lambda v: v.tensor_add(
        v8(sfc), v8(sc_tm),
        c_p["cbbc"][:, :].unsqueeze(1).broadcast_to([128, 16, 8])))

    def top1_mask(src_view, io_view, iom_view, n, w, out_mask_view):
        rv(lambda v: v.reduce_max(m1[:, 0:n], src_view, axis=AX.X))
        rv(lambda v: v.tensor_tensor(out_mask_view, src_view,
                                     bc(m1[:, 0:n], n, w), ALU.is_ge))
        rv(lambda v: v.tensor_tensor(r2[:, 0:n * w].rearrange(
            "p (a b) -> p a b", b=w), out_mask_view, iom_view, ALU.mult))
        rv(lambda v: v.tensor_scalar(r2[:, 0:n * w], r2[:, 0:n * w],
                                     BIGI, None, ALU.add))
        rv(lambda v: v.tensor_reduce(i1[:, 0:n], r2[:, 0:n * w].rearrange(
            "p (a b) -> p a b", b=w), axis=AX.X, op=ALU.min))
        rv(lambda v: v.tensor_tensor(out_mask_view, io_view,
                                     bc(i1[:, 0:n], n, w), ALU.is_equal))

    for gi, gdst in ((0, gs0), (1, gs1)):
        gview = v8(sfc)[:, :, gi * 4:(gi + 1) * 4]
        iov = v8(c_iota8[:, 0:128])[:, :, gi * 4:(gi + 1) * 4]
        iomv = v8(c_iotam[:, 0:128])[:, :, gi * 4:(gi + 1) * 4]
        f1v = v4(r1[:, 0:64])
        top1_mask(gview, iov, iomv, 16, 4, f1v)
        rv(lambda v, gview=gview, f1v=f1v: v.scalar_tensor_tensor(
            v4(r2[:, 0:64]), f1v, -BIGM, gview, ALU.mult, ALU.add))
        rv(lambda v: v.reduce_max(i1[:, 0:16], v4(r2[:, 0:64]), axis=AX.X))
        rv(lambda v, gdst=gdst: v.tensor_add(gdst[:, 0:16], m1[:, 0:16],
                                             i1[:, 0:16]))
    rv(lambda v: v.tensor_tensor(gs0[:, 0:16], gs0[:, 0:16], gs1[:, 0:16],
                                 ALU.is_ge))
    rv(lambda v: v.tensor_tensor(
        v8(sfc)[:, :, 0:4], v8(sfc)[:, :, 0:4], bc(gs0[:, 0:16], 16, 4),
        ALU.mult))
    rv(lambda v: v.tensor_scalar(gs1[:, 0:16], gs0[:, 0:16], -1.0, 1.0,
                                 ALU.mult, ALU.add))
    rv(lambda v: v.tensor_tensor(
        v8(sfc)[:, :, 4:8], v8(sfc)[:, :, 4:8], bc(gs1[:, 0:16], 16, 4),
        ALU.mult))
    f1v = v8(r1)
    top1_mask(v8(sfc), v8(c_iota8[:, 0:128]), v8(c_iotam[:, 0:128]),
              16, 8, f1v)
    rv(lambda v: v.scalar_tensor_tensor(v8(r2), f1v, -BIGM, v8(sfc),
                                        ALU.mult, ALU.add))
    f2v = v8(wts_tm)
    top1_mask(v8(r2), v8(c_iota8[:, 0:128]), v8(c_iotam[:, 0:128]),
              16, 8, f2v)
    rv(lambda v: v.tensor_add(v8(wts_tm), v8(wts_tm), f1v))
    rv(lambda v: v.tensor_mul(v8(wts_tm), v8(wts_tm), v8(sc_tm)))
    rv(lambda v: v.reduce_sum(dnm[:, 0:16], v8(wts_tm), axis=AX.X))
    rv(lambda v: v.tensor_scalar(dnm[:, 0:16], dnm[:, 0:16], 1e-20, None,
                                 ALU.add))
    B.step("scalar", lambda s: s.activation(dnm[:, 0:16], dnm[:, 0:16],
                                            AF.Ln),
           reads=[b_r], writes=[b_r])
    B.step("scalar", lambda s: s.activation(dnm[:, 0:16], dnm[:, 0:16],
                                            AF.Exp, scale=-1.0),
           reads=[b_r], writes=[b_r])
    wts_em = sfc.rearrange("p (e a) -> p a e", a=16)
    rv(lambda v: v.tensor_tensor(wts_em, v8(wts_tm),
                                 bc(dnm[:, 0:16], 16, 8), ALU.mult),
       writes=[b_wts])
    wEb = P4.bitcast(BF16)[:, 9632:9760]
    wcmT = P4.bitcast(BF16)[:, 9760:9888]
    B.step("vector", lambda v: v.tensor_copy(wEb[:, 0:128], sfc),
           reads=[b_wts], writes=[b_wts])
    B.step("tensor", lambda t: t.transpose(
        pb[1].bitcast(BF16)[:, 0:128], wEb[:, 0:128], c_identb[:]),
        reads=[b_wts, b_const], writes=[pbuf[1]])
    B.step("scalar", lambda s: s.copy(wcmT[:, 0:128],
                                      pb[1].bitcast(BF16)[:, 0:128]),
        reads=[pbuf[1]], writes=[b_wts])
    if "wts" in taps:
        dma("sync", tap["wts"], sfc, Buf("tapwts"), rbufs=[b_wts])

    mark('f1_router')
    b_lat = Buf("lat").retag(b_ro)
    lat_l = [P2[:, l * 2048:(l + 1) * 2048] for l in range(4)]
    for l in range(4):
        for tb in range(TBN):
            bk = 2 + (l * TBN + tb) % 2

            def mmf1(t, l=l, tb=tb, bk=bk):
                last = None
                for c in range(NCH):
                    last = t.matmul(
                        pb[bk][:],
                        fc1_ap[:, c * 512 + l * 128:c * 512 + (l + 1) * 128],
                        y2_c[c][:, tb * 512:(tb + 1) * 512],
                        start=(c == 0), stop=(c == NCH - 1))
                return last
            B.step("tensor", mmf1, reads=[b_fc1, b_y2], writes=[pbuf[bk]])
            B.step("scalar", lambda s, l=l, tb=tb, bk=bk: s.copy(
                lat_l[l][:, tb * 512:(tb + 1) * 512], pb[bk][:]),
                reads=[pbuf[bk]], writes=[b_lat])
    if "lat" in taps:
        for l in range(4):
            dma("sync", tap["lat"][l], lat_l[l], Buf(f"taplat{l}"),
                rbufs=[b_lat])

    mark('f1_fc1')
    b_comb = Buf("comb").retag(b_sc, *b_xP[0], *b_xP[1])
    comb_l = [P3[:, l * 2048:(l + 1) * 2048] for l in range(4)]
    b_h1w = Buf("h1w").retag(b_lat)
    h1w_i = [P2[:, 8192 + i * 2048:8192 + (i + 1) * 2048] for i in range(4)]
    P4b = P4.bitcast(BF16)
    b_wrowd = [Buf("wrowd0").retag(b_xq, b_r),
               Buf("wrowd1").retag(b_xq, b_r)]
    b_wbc = [Buf("wbc0b").retag(b_xq, b_r),
             Buf("wbc1b").retag(b_xq, b_r)]
    wrow_ap2 = [P4b[:, 0:2048], P4b[:, 2048:4096]]
    wbc_ap2 = [P4b[:, 4096:6144], P4b[:, 6144:8192]]
    tmps_ap = P3[:, 12288:16384]
    tmpA = [tmps_ap[:, j * 512:(j + 1) * 512] for j in range(4)]
    tmpB = [tmps_ap[:, 2048 + j * 512:2048 + (j + 1) * 512]
            for j in range(4)]
    _tmprt = [b_sc] + [b for tb in range(TBN) for b in (b_xP[2][tb],
                                                        b_xP[3][tb])]
    b_tmpA = [Buf(f"tmpA{j}").retag(*_tmprt) for j in range(4)]
    b_tmpB = [Buf(f"tmpB{j}").retag(*_tmprt) for j in range(4)]

    for e in range(NE):
        si = e % 2
        dma("sync", wrow_ap2[si][0:1, 0:2048],
            wcmT[16 * e:16 * e + 16, 0:128], b_wrowd[si], rbufs=[b_wts])
        for tb in range(TBN):
            def mmb(t, tb=tb, si=si):
                return t.matmul(pb[1][:], c_onesrowb[:],
                                wrow_ap2[si][0:1, tb * 512:(tb + 1) * 512],
                                start=True, stop=True)
            B.step("tensor", mmb, reads=[b_wrowd[si], b_const],
                   writes=[pbuf[1]])
            B.step("scalar", lambda s, tb=tb, si=si: s.copy(
                wbc_ap2[si][:, tb * 512:(tb + 1) * 512], pb[1][:]),
                reads=[pbuf[1]], writes=[b_wbc[si]])
        if e == 0 and "w1sb" in taps:
            dma("sync", tap["w1sb"], w1_ap[0], Buf("tapw1"),
                rbufs=[w1_sb[0]])
        if e == 0 and "wbc0" in taps:
            dma("sync", tap["wbc0"], wbc_ap2[0], Buf("tapwbc0"),
                rbufs=[b_wbc[0]])
        for i in range(4):
            for tb in range(TBN):
                bk = (0, 1, 4, 5)[(i * TBN + tb) % 4]

                def mmh(t, i=i, tb=tb, bk=bk, si=si):
                    last = None
                    for l in range(4):
                        last = t.matmul(
                            pb[bk][:],
                            w1_ap[si][:, l * 512 + i * 128:
                                      l * 512 + (i + 1) * 128],
                            lat_l[l][:, tb * 512:(tb + 1) * 512],
                            start=(l == 0), stop=(l == 3))
                    return last
                B.step("tensor", mmh, reads=[w1_sb[si], b_lat],
                       writes=[pbuf[bk]])
                j = (i * TBN + tb) % 4
                B.step("scalar", lambda s, bk=bk, j=j: s.activation(
                    tmpA[j], pb[bk][:], AF.Relu),
                    reads=[pbuf[bk]], writes=[b_tmpA[j]])
                if e == 0 and i == 0 and tb == 0 and "h1r00" in taps:
                    dma("sync", tap["h1r00"], tmpA[0], Buf("taph1r"),
                        rbufs=[b_tmpA[0]])
                B.step("vector", lambda v, j=j: v.tensor_mul(
                    tmpB[j], tmpA[j], tmpA[j]),
                    reads=[b_tmpA[j]], writes=[b_tmpB[j]])
                if e == 0 and i == 0 and tb == 0 and "h1r00" in taps:
                    dma("sync", tap["h1s00"], tmpB[0], Buf("taph1s"),
                        rbufs=[b_tmpB[0]])
                B.step("vector", lambda v, i=i, tb=tb, j=j, si=si:
                       v.tensor_mul(
                           h1w_i[i][:, tb * 512:(tb + 1) * 512], tmpB[j],
                           wbc_ap2[si][:, tb * 512:(tb + 1) * 512]),
                       reads=[b_tmpB[j], b_wbc[si]], writes=[b_h1w])
        if e == 0 and "h1w0" in taps:
            for i in range(4):
                dma("sync", tap["h1w0"][i], h1w_i[i], Buf(f"taph1w{i}"),
                    rbufs=[b_h1w])
        for l in range(4):
            for tb in range(TBN):
                bk = (2, 3, 6, 7)[(l * TBN + tb) % 4]

                def mmc(t, l=l, tb=tb, bk=bk, si=si):
                    last = None
                    for i in range(4):
                        last = t.matmul(
                            pb[bk][:],
                            w2_ap[si][:, i * 512 + l * 128:
                                      i * 512 + (l + 1) * 128],
                            h1w_i[i][:, tb * 512:(tb + 1) * 512],
                            start=(i == 0), stop=(i == 3))
                    return last
                B.step("tensor", mmc, reads=[w2_sb[si], b_h1w],
                       writes=[pbuf[bk]])
                if e == 0:
                    B.step("scalar", lambda s, l=l, tb=tb, bk=bk: s.copy(
                        comb_l[l][:, tb * 512:(tb + 1) * 512], pb[bk][:]),
                        reads=[pbuf[bk]], writes=[b_comb])
                else:
                    B.step("vector", lambda v, l=l, tb=tb, bk=bk:
                           v.tensor_add(
                               comb_l[l][:, tb * 512:(tb + 1) * 512],
                               comb_l[l][:, tb * 512:(tb + 1) * 512],
                               pb[bk][:]),
                           reads=[pbuf[bk], b_comb], writes=[b_comb])
        if e + 2 < NE:
            load_expert(e + 2)
    if "comb" in taps:
        for l in range(4):
            dma("sync", tap["comb"][l], comb_l[l], Buf(f"tapcomb{l}"),
                rbufs=[b_comb])

    mark('f1_experts')
    upg_ap = [PWb[:, 8256:12352], PWb[:, 12352:16448]]
    b_upg = [Buf("upg0").retag(w1_sb[0], w2_sb[0]),
             Buf("upg1").retag(w1_sb[1], w2_sb[1])]
    dng_ap = [PWb[:, 0:4096], P2[:, 8192:12288]]
    b_dng = [Buf("dng0").retag(b_gw, b_fc1),
             Buf("dng1").retag(b_h1w)]

    b_upq = Buf("upq").retag(b_ekv,
                             *[b for c in range(4, 8) for b in b_xP[c]])
    upq_ds = [P5[:, ds * 512:(ds + 1) * 512] for ds in range(32)]
    b_fin = Buf("fin").retag(b_scanA, b_scanB, b_xq, b_wrowd[0],
                             b_wrowd[1], b_wbc[0], b_wbc[1], b_xscr2)
    fin_o = [P4[:, o * 512:(o + 1) * 512] for o in range(8)]
    b_x1q = Buf("x1q").retag(b_sctm, b_wts, b_r, b_rt)
    x1q_ap = [P4[:, 4096:4608], P4[:, 4608:5120]]
    b_ost2 = [Buf("ostage0").retag(b_stat), Buf("ostage1").retag(b_stat)]
    ost_ap2 = [P4[:, 5120:6144], P4[:, 6144:7168]]
    b_tmp3 = Buf("f2tmp").retag(*b_tmpA, *b_tmpB)
    tmp3 = sqt2_ap[:, 1024:1536]

    upg_src = upTb_d.rearrange("g d p l e -> g p d (l e)")
    dng_src = downTb_d.rearrange("o sc p e -> o p sc e")

    last_out_tok = None
    for tb in range(TBN):
        for g in range(8):
            gb = g % 2
            dma("sync",
                upg_ap[gb].rearrange("p (d le) -> p d le", le=512),
                upg_src[g], b_upg[gb], rbufs=[b_upTb])
            for l in range(4):
                ds = g * 4 + l
                bk = (ds % 2)

                def mmu(t, g=g, l=l, tb=tb, bk=bk, gb=gb):
                    last = None
                    for dd in range(8):
                        last = t.matmul(
                            pb[bk][:],
                            upg_ap[gb][:, (dd * 4 + l) * 128:
                                       (dd * 4 + l + 1) * 128],
                            y2_c[dd][:, tb * 512:(tb + 1) * 512],
                            start=(dd == 0), stop=(dd == 7))
                    return last
                B.step("tensor", mmu, reads=[b_upg[gb], b_y2],
                       writes=[pbuf[bk]])
                B.step("scalar", lambda s, bk=bk: s.activation(
                    tmp3, pb[bk][:], AF.Relu),
                    reads=[pbuf[bk]], writes=[b_tmp3])
                B.step("vector", lambda v, ds=ds: v.tensor_mul(
                    upq_ds[ds], tmp3, tmp3),
                    reads=[b_tmp3], writes=[b_upq])
        for o in range(8):
            ob = o % 2
            dma("sync",
                dng_ap[ob].rearrange("p (sc e) -> p sc e", e=128),
                dng_src[o], b_dng[ob], rbufs=[b_dnTb])
            bk = 2 + o % 2

            def mmd(t, o=o, tb=tb, bk=bk, ob=ob):
                last = None
                for l in range(4):
                    last = t.matmul(
                        pb[bk][:],
                        fc2_ap[:, l * 1024 + o * 128:
                               l * 1024 + (o + 1) * 128],
                        comb_l[l][:, tb * 512:(tb + 1) * 512],
                        start=(l == 0), stop=False)
                for sc in range(32):
                    last = t.matmul(
                        pb[bk][:],
                        dng_ap[ob][:, sc * 128:(sc + 1) * 128],
                        upq_ds[sc], start=False, stop=(sc == 31))
                return last
            B.step("tensor", mmd,
                   reads=[b_fc2, b_comb, b_dng[ob], b_upq],
                   writes=[pbuf[bk]])
            xi = o % 2
            dma("scalar", x1q_ap[xi],
                x1_rows[o][:, tb * 512:(tb + 1) * 512], b_x1q,
                rbufs=[b_x1])
            B.step("vector", lambda v, o=o, xi=xi, bk=bk: v.tensor_add(
                fin_o[o], x1q_ap[xi], pb[bk][:]),
                reads=[b_x1q, pbuf[bk]], writes=[b_fin])
        for ti in range(4):
            ob2 = ti % 2
            bk0 = 4 + 2 * ob2
            for o in range(8):
                bk = bk0 + (o // 4)

                def emt2(t, o=o, ti=ti, bk=bk):
                    return t.transpose(
                        pb[bk][:, (o % 4) * 128:(o % 4 + 1) * 128],
                        fin_o[o][:, ti * 128:(ti + 1) * 128],
                        c_identf[:])
                B.step("tensor", emt2, reads=[b_fin, b_const],
                       writes=[pbuf[bk]])
            B.step("scalar", lambda s, ob2=ob2, bk0=bk0: s.copy(
                ost_ap2[ob2][:, 0:512], pb[bk0][:]),
                reads=[pbuf[bk0]], writes=[b_ost2[ob2]])
            B.step("scalar", lambda s, ob2=ob2, bk0=bk0: s.copy(
                ost_ap2[ob2][:, 512:1024], pb[bk0 + 1][:]),
                reads=[pbuf[bk0 + 1]], writes=[b_ost2[ob2]])
            t0 = tb * 512 + ti * 128
            last_out_tok = dma("gpsimd", out[t0:t0 + 128, :],
                               ost_ap2[ob2],
                               Buf(f"outw{tb}_{ti}"), rbufs=[b_ost2[ob2]])

    mark('f2_shared')
    fin_sync = B.dcnt["sync"] * 16
    fin_scalar = B.dcnt["scalar"] * 16
    fin_gpsimd = B.dcnt["gpsimd"] * 16

    def final_waits(eng):
        eng.wait_ge(B.semobj["d_sync"], fin_sync)
        eng.wait_ge(B.semobj["d_scalar"], fin_scalar)
        eng.wait_ge(B.semobj["d_gpsimd"], fin_gpsimd)
    B.raw("sync", final_waits)

    with nc.Block() as block:
        @block.sync
        def _(eng):
            for fn in B.q["sync"]:
                fn(eng)

        @block.tensor
        def _(eng):
            for fn in B.q["tensor"]:
                fn(eng)

        @block.vector
        def _(eng):
            for fn in B.q["vector"]:
                fn(eng)

        @block.scalar
        def _(eng):
            for fn in B.q["scalar"]:
                fn(eng)

        @block.gpsimd
        def _(eng):
            for fn in B.q["gpsimd"]:
                fn(eng)

    return nc


def _prep_inputs(ins):
    f = np.ascontiguousarray
    w = {}
    w["wkT"] = f(np.asarray(ins["L0_attn_Wk"]).T)
    w["wvT"] = f(np.asarray(ins["L0_attn_Wv"]).T)
    w["wrT"] = f(np.asarray(ins["L0_attn_Wr"]).T)
    w["woT"] = f(np.asarray(ins["L0_attn_Wo"]).T)
    w["gateT"] = f(np.asarray(ins["L0_ffn_gate"]).T)
    w["fc1T"] = f(np.asarray(ins["L0_ffn_fc1_w"]).T)
    w["fc2T"] = f(np.asarray(ins["L0_ffn_fc2_w"]).T)
    w["upT"] = f(np.asarray(ins["L0_ffn_up_w"]).T)
    dw = np.asarray(ins["L0_ffn_down_w"])
    w["downTr"] = f(dw.reshape(8, 128, DS).transpose(0, 2, 1))
    w["w1T"] = f(np.asarray(ins["L0_ffn_w1"]).transpose(0, 2, 1))
    w["w2T"] = f(np.asarray(ins["L0_ffn_w2"]).transpose(0, 2, 1))
    for nm, key in (("tdcm", "L0_attn_time_decay"),
                    ("tfcm", "L0_attn_time_first")):
        w[nm] = f(np.asarray(key and ins[key]).reshape(8, 128).T)
    for nm, key in (("tmkcm", "L0_attn_time_mix_k"),
                    ("tmvcm", "L0_attn_time_mix_v"),
                    ("tmrcm", "L0_attn_time_mix_r")):
        w[nm] = f(np.asarray(ins[key]).reshape(8, 128).T)
    cb = np.asarray(ins["L0_ffn_corr_bias"]).astype(np.float32)
    w["cbbc"] = f(np.broadcast_to(cb[None, :], (128, 8)))
    w["identf"] = np.eye(128, dtype=np.float32)
    w["identb"] = np.eye(128, dtype=ml_dtypes.bfloat16)
    w["ones128b"] = np.ones((128, 128), dtype=ml_dtypes.bfloat16)
    w["onesrowb"] = np.ones((1, 128), dtype=ml_dtypes.bfloat16)
    io = np.tile(np.arange(8, dtype=np.float32), 16)[None, :]
    w["iota8"] = f(np.broadcast_to(io, (128, 128)))
    w["iotam"] = f(np.broadcast_to(io - BIGI, (128, 128)))
    w["epscol"] = np.full((128, 1), 1e-5, dtype=np.float32)
    return w


def run(inputs, taps=(), cores=8):
    nc = build_program(taps)
    shared = _prep_inputs(inputs)
    xfull = np.asarray(inputs["L0_attn_x"], dtype=np.float32)
    in_maps = []
    for b in range(cores):
        m = dict(shared)
        m["x"] = np.ascontiguousarray(xfull[b])
        in_maps.append(m)
    res = run_bass_kernel_spmd(nc, in_maps, core_ids=list(range(cores)))
    return res


def kernel(**inputs):
    res = run(inputs, taps=(), cores=8)
    return np.stack([r["out"] for r in res.results]).astype(np.float32)
